# version 01
# baseline (speedup 1.0000x reference)
import math
import numpy as np
import ml_dtypes
from contextlib import ExitStack

import concourse.bass as bass
import concourse.mybir as mybir
from concourse.bass_utils import run_bass_kernel_spmd

F32 = mybir.dt.float32
BF16 = mybir.dt.bfloat16
AF = mybir.ActivationFunctionType
ALU = mybir.AluOpType
AX = mybir.AxisListType

MAGIC = float(np.float32(1.5 * 2 ** 23))
EPS = 1e-5

T = 2048
H = 2048
HC = 1024
HEADS = 16
DH = 64
WIN = 512
I8K = 8192
TC = 512
NCH = T // TC

BF = ml_dtypes.bfloat16


def hadamard(n):
    h = np.array([[1.0]])
    while h.shape[0] < n:
        h = np.block([[h, h], [h, -h]])
    return h.astype(np.float32)


def host_constants():
    c = {}
    c["I128f"] = np.eye(128, dtype=np.float32)
    c["I128b"] = np.eye(128).astype(BF)
    c["H128f"] = hadamard(128)
    c["J128f"] = np.eye(128, dtype=np.float32)[::-1].copy()
    c["Mrow"] = np.full((1, 128), MAGIC, dtype=np.float32)
    for n, m in ((2048, 16), (8192, 64)):
        tpb = 128 // m
        hm = hadamard(m)
        bmax = np.zeros((128, 128), dtype=np.float32)
        bev = np.zeros((128, 128), dtype=np.float32)
        for a in range(m):
            for r in range(tpb):
                q = a * tpb + r
                for a2 in range(m):
                    bmax[q, r * m + a2] = hm[a, a2]
                    bev[q, a2 * tpb + r] = hm[a, a2]
        c[f"Bmax{n}"] = bmax
        c[f"Bev{n}"] = bev
        p1 = np.zeros((128, 128), dtype=np.float32)
        e1 = np.zeros((128, 128 // tpb), dtype=np.float32)
        for t in range(128):
            for a in range(m):
                p1[t, a * tpb + (t % tpb)] = 1.0
            e1[t, t // tpb] = 1.0
        c[f"P{n}"] = p1
        c[f"E{n}"] = e1
    peo = np.zeros((128, 128), dtype=np.float32)
    for q in range(128):
        peo[2 * (q % 64) + (q // 64), q] = 1.0
    c["PermEO"] = peo
    c["PermEOT"] = peo.T.copy()
    qoff = np.arange(128)[:, None]
    koff = np.arange(512)[None, :]
    masks = np.zeros((4, 128, 512), dtype=np.float32)
    for qc in range(4):
        masks[qc] = np.where(koff <= qoff + qc * 128, 0.0, -1e9)
    c["cmask"] = masks.astype(BF)
    return c


CONST_SPECS = [
    ("I128f", (128, 128), F32), ("I128b", (128, 128), BF16),
    ("H128f", (128, 128), F32), ("J128f", (128, 128), F32),
    ("Mrow", (1, 128), F32),
    ("Bmax2048", (128, 128), F32), ("Bev2048", (128, 128), F32),
    ("Bmax8192", (128, 128), F32), ("Bev8192", (128, 128), F32),
    ("P2048", (128, 128), F32), ("E2048", (128, 16), F32),
    ("P8192", (128, 128), F32), ("E8192", (128, 64), F32),
    ("PermEO", (128, 128), F32), ("PermEOT", (128, 128), F32),
    ("cmask", (4, 128, 512), BF16),
]


class Sem:
    __slots__ = ("h", "n", "name")

    def __init__(self, k, name):
        self.h = k.es.enter_context(k.nc.semaphore(name))
        self.n = 0
        self.name = name

    def inc(self, inst, k=1):
        inst.then_inc(self.h, k)
        self.n += k
        return self.n

    def dma(self, inst):
        return self.inc(inst, 16)


class K:
    def __init__(self, nc):
        self.nc = nc
        self.es = ExitStack()
        self._lw = {}
        self.c = {}

    def sem(self, name):
        return Sem(self, name)

    def sb(self, name, shape, dtype):
        return self.es.enter_context(self.nc.sbuf_tensor(name, list(shape), dtype))

    def psum(self, name, shape, dtype=F32):
        return self.es.enter_context(self.nc.psum_tensor(name, list(shape), dtype))

    def wait(self, eng, engname, sem, val=None):
        val = sem.n if val is None else val
        if val <= 0:
            return
        key = (engname, sem.name)
        if self._lw.get(key, -1) >= val:
            return
        eng.wait_ge(sem.h, val)
        self._lw[key] = val


EO = np.array([2 * (q % 64) + q // 64 for q in range(128)])
EOINV = np.argsort(EO)


def eo_read_dims(ncols, colstep=1):
    return [[2048, 64], [64 * 2048, 2], [colstep, ncols]]


def AP(t, offset, dims):
    if isinstance(t, bass.AP):
        th = t.tensor
    elif hasattr(t, "ap") and callable(getattr(t, "ap")):
        th = t.ap().tensor
    else:
        th = t
    return bass.AP(th, offset, dims)


def emit_load_consts(k, blk, dram_consts):
    sD = k.sem("c_dma")
    for name, shape, dt in CONST_SPECS:
        if name == "cmask":
            continue
        k.c[name] = k.sb("c_" + name, shape, dt)
    k.c["ones1"] = k.sb("c_ones1", (1, 128), F32)
    k.c["ones1b"] = k.sb("c_ones1b", (1, 128), BF16)
    k.c["onescol"] = k.sb("c_onescol", (128, 1), F32)
    k.c["mgc"] = k.sb("c_mgc", (128, 6), F32)
    k.c["ones128f"] = k.sb("c_ones128f", (128, 128), F32)

    def g(gp):
        for name, shape, dt in CONST_SPECS:
            if name == "cmask":
                continue
            sD.dma(gp.dma_start(k.c[name][:, :], dram_consts[name]))
        gp.memset(k.c["ones1"][:, :], 1.0)
        gp.memset(k.c["ones1b"][:, :], 1.0)
        gp.memset(k.c["onescol"][:, :], 1.0)
        gp.memset(k.c["mgc"][:, 0:1], MAGIC)
        gp.memset(k.c["mgc"][:, 1:2], -MAGIC)
        gp.memset(k.c["mgc"][:, 2:3], EPS)
        gp.memset(k.c["mgc"][:, 3:4], 0.125)
        gp.memset(k.c["mgc"][:, 4:5], 1.0)
        gp.memset(k.c["mgc"][:, 5:6], float(np.float32(1.0 / math.sqrt(2048.0))))
        gp.memset(k.c["ones128f"][:, :], 1.0)
        k.wait(gp, "gpsimd", sD)
    blk.gpsimd(g)
    return sD


def emit_prep(k, blk, wsrcs, scratches, wsid):
    sW = k.sem("p_w")
    sS = k.sem("p_s")
    sV = k.sem("p_v")
    sT = k.sem("p_t")
    sO = k.sem("p_o")
    es2 = ExitStack()
    lsb = lambda n, sh, dt: es2.enter_context(k.nc.sbuf_tensor(n, list(sh), dt))
    lps = lambda n, sh, dt: es2.enter_context(k.nc.psum_tensor(n, list(sh), dt))
    wt = [lsb(f"p_wt{i}", (128, 2048), F32) for i in range(2)]
    junk = lsb("p_junk", (128, 2048), F32)
    t1 = lsb("p_t1", (128, 2048), F32)
    t2 = lsb("p_t2", (128, 2048), F32)
    tern = lsb("p_tern", (128, 2048), BF16)
    acc = lsb("p_acc", (128, 64), F32)
    colsum = lsb("p_colsum", (128, 1), F32)
    ssc = lsb("p_ssc", (1, 2), F32)
    evb = [lsb(f"p_evb{i}", (128, 512), BF16) for i in range(2)]
    ptr = [lps(f"p_ptr{i}", (128, 512), BF16) for i in range(2)]
    psc = lps("p_psc", (128, 2), F32)

    done = {}
    absmark = {}
    roundmark = {}
    clampmark = {}
    trmark = {}
    evacmark = {}
    outmark = {}
    wt_free = [0, 0]

    for j, (tag, (wsrc, n_out, n_in)) in enumerate(wsrcs.items()):
        scratch = scratches[tag]
        n_ot = n_out // 128
        n_cg = n_in // 2048
        ntile = n_ot * n_cg
        G = j * 1000

        def tile_src(it):
            ot, cg = divmod(it, n_cg)
            return wsrc[ot * 128:(ot + 1) * 128, cg * 2048:(cg + 1) * 2048]

        def blkrow(it, j16):
            ot, cg = divmod(it, n_cg)
            if n_cg == 1:
                return ot * 16 + j16
            return (cg * 16 + j16) * 16 + ot

        assert ntile <= 64
        for it in range(ntile):
            b = it % 2

            def dmain(s, it=it, b=b):
                k.wait(s, "sync", sS, wt_free[b])
                sW.dma(s.dma_start(wt[b][:, :], tile_src(it)))
            blk.sync(dmain)

            def absacc(s, it=it, b=b):
                k.wait(s, "scalar", sW)
                absmark[G + it] = sS.inc(s.activation(
                    junk[:, :], wt[b][:, :], AF.Abs,
                    accum_out=acc[:, it:it + 1]))
                wt_free[b] = absmark[G + it]
            blk.scalar(absacc)

        def redscale(v):
            k.wait(v, "vector", sS)
            sV.inc(v.tensor_reduce(colsum[:, :], acc[:, 0:ntile], axis=AX.X,
                                   op=ALU.add))
        blk.vector(redscale)

        def xsum(t):
            k.wait(t, "tensor", sV)
            sT.inc(t.matmul(psc[:1, 0:1], colsum[:, :], k.c["onescol"][:, :],
                            start=True, stop=True))
        blk.tensor(xsum)

        def finscale(v, n_out=n_out, n_in=n_in):
            k.wait(v, "vector", sT)
            sV.inc(v.tensor_copy(ssc[:1, 0:1], psc[:1, 0:1]))
            k.wait(v, "vector", sV)
            sV.inc(v.tensor_scalar(ssc[:1, 0:1], ssc[:1, 0:1],
                                   1.0 / float(n_out * n_in), EPS,
                                   op0=ALU.mult, op1=ALU.max))
            k.wait(v, "vector", sV)
            sV.inc(v.reciprocal(ssc[:1, 1:2], ssc[:1, 0:1]))
        blk.vector(finscale)

        def bcscale(t):
            k.wait(t, "tensor", sV)
            sT.inc(t.matmul(psc[:, 0:2], k.c["ones1"][:1, :], ssc[:1, 0:2],
                            start=True, stop=True))
        blk.tensor(bcscale)

        def evscale(v, j=j):
            k.wait(v, "vector", sT)
            sV.inc(v.tensor_copy(wsid[:, 2 * j:2 * j + 2], psc[:, 0:2]))
        blk.vector(evscale)
        scale_ready = sV.n

        for it in range(ntile):
            b = it % 2

            def dmain2(s, it=it, b=b):
                k.wait(s, "sync", sS, wt_free[b])
                sW.dma(s.dma_start(wt[b][:, :], tile_src(it)))
            blk.sync(dmain2)

            def round_(s, it=it, b=b, j=j):
                k.wait(s, "scalar", sW)
                k.wait(s, "scalar", sV, scale_ready)
                if (G + it - 1) in clampmark:
                    k.wait(s, "scalar", sV, clampmark[G + it - 1])
                roundmark[G + it] = sS.inc(s.activation(
                    t1[:, :], wt[b][:, :], AF.Identity,
                    bias=k.c["mgc"][:, 0:1], scale=wsid[:, 2 * j + 1:2 * j + 2]))
                wt_free[b] = roundmark[G + it]
            blk.scalar(round_)

            def clampcast(v, it=it):
                k.wait(v, "vector", sS, roundmark[G + it])
                if (G + it - 1, 3) in trmark:
                    k.wait(v, "vector", sT, trmark[(G + it - 1, 3)])
                v.tensor_scalar(t2[:, :], t1[:, :], MAGIC - 1.0, MAGIC + 1.0,
                                op0=ALU.max, op1=ALU.min)
                clampmark[G + it] = sV.inc(v.tensor_scalar(
                    tern[:, :], t2[:, :], -MAGIC, None, op0=ALU.add))
            blk.vector(clampcast)

            for g4 in range(4):
                pb = g4 % 2
                prev = (G + it, g4 - 2) if g4 >= 2 else (G + it - 1, g4 + 2)

                def trs(t, it=it, g4=g4, pb=pb, prev=prev):
                    k.wait(t, "tensor", sV, clampmark[G + it])
                    if prev in evacmark:
                        k.wait(t, "tensor", sV, evacmark[prev])
                    mm = None
                    for q in range(4):
                        mm = t.matmul(
                            ptr[pb][:, q * 128:(q + 1) * 128],
                            tern[:, (g4 * 4 + q) * 128:(g4 * 4 + q + 1) * 128],
                            k.c["I128b"][:, :],
                            start=(q == 0), stop=(q == 3), is_transpose=True)
                    trmark[(G + it, g4)] = sT.inc(mm)
                blk.tensor(trs)

                def evac(v, it=it, g4=g4, pb=pb, prev=prev):
                    k.wait(v, "vector", sT, trmark[(G + it, g4)])
                    if prev in outmark:
                        k.wait(v, "vector", sO, outmark[prev])
                    evacmark[(G + it, g4)] = sV.inc(
                        v.tensor_copy(evb[pb][:, :], ptr[pb][:, :]))
                blk.vector(evac)

                def store(gp, it=it, g4=g4, pb=pb, scratch=scratch,
                          blkrow=blkrow):
                    k.wait(gp, "gpsimd", sV, evacmark[(G + it, g4)])
                    rows = [blkrow(it, g4 * 4 + q) for q in range(4)]
                    stride = rows[1] - rows[0]
                    assert rows[2] - rows[1] == stride and rows[3] - rows[2] == stride
                    src = AP(evb[pb], 0, [[512, 128], [128, 4], [1, 128]])
                    dst = AP(scratch, rows[0] * 128 * 128,
                             [[128, 128], [stride * 128 * 128, 4], [1, 128]])
                    outmark[(G + it, g4)] = sO.dma(gp.dma_start(dst, src))
                blk.gpsimd(store)
        done[tag] = sO.n
    es2.close()
    return done, [sW, sS, sV, sT, sO]


def alloc_ffn(k, es=None, sfx=""):
    if es is not None:
        _sb = lambda n, sh, dt: es.enter_context(
            k.nc.sbuf_tensor(n + sfx, list(sh), dt))
        _ps = lambda n, sh, dt=F32: es.enter_context(
            k.nc.psum_tensor(n + sfx, list(sh), dt))
        sav = (k.sb, k.psum)
        k.sb, k.psum = _sb, _ps
        try:
            return _alloc_ffn_inner(k)
        finally:
            k.sb, k.psum = sav
    return _alloc_ffn_inner(k)


def _alloc_ffn_inner(k):
    B = {}
    B["z8"] = k.sb("f_z8", (128, 32768), F32)
    B["yq"] = B["z8"].bitcast(BF16)
    B["z2"] = k.sb("f_z2", (128, 2048), F32)
    B["lnf"] = k.sb("f_lnf", (128, 2048), F32)
    B["xt"] = [k.sb("f_xt0", (128, 2048), F32)]
    B["xq"] = k.sb("f_xq", (128, 8192), BF16)
    B["st"] = k.sb("f_st", (128, 40), F32)
    B["sg"] = k.sb("f_sg", (128, 512), F32)
    B["xo"] = k.sb("f_xo", (128, 512), F32)
    B["xres"] = k.sb("f_xres", (128, 512), F32)
    B["mxp"] = k.sb("f_mxp", (128, 512), F32)
    B["sz8"] = k.sb("f_sz8", (128, 16), F32)
    B["su8"] = k.sb("f_su8", (128, 256), F32)
    B["em"] = k.sb("f_em", (128, 64), F32)
    B["abc"] = k.sb("f_abc", (128, 512), F32)
    B["mxp2"] = B["lnf"].bitcast(F32)
    B["dg"] = None
    B["em2"] = None
    B["su2"] = None
    B["wacol"] = k.sb("f_wacol", (128, 4), F32)
    B["wg"] = [k.sb(f"f_wgb{i}", (128, 2048), BF16) for i in range(2)]
    B["wu"] = [k.sb(f"f_wub{i}", (128, 2048), BF16) for i in range(2)]
    B["wd"] = [k.sb(f"f_wdb{i}", (128, 1024), BF16) for i in range(2)]
    B["pp"] = [k.psum(f"f_pp{i}", (128, 512), F32) for i in range(8)]
    return B


def _ffn_stageA(k, blk, B, x_in, sems, mk, c):
    c_ = k.c
    fW, fG, fT, fV, fS = (sems[x] for x in ("fW", "fG", "fT", "fV", "fS"))
    pp = B["pp"]
    SQ2 = float(np.float32(math.sqrt(2048.0)))
    xq_evac_marks = []
    for tsub in range(4):
        b = 0
        trow = c * 512 + tsub * 128
        stc = tsub * 10

        def ldx(g, b=b, trow=trow):
            k.wait(g, "gpsimd", fT, mk["xt_free"][b])
            fG.dma(g.dma_start(B["xt"][b][:, :], x_in[trow:trow + 128, :]))
        blk.gpsimd(ldx)
        ldx_mark = fG.n

        def stats(v, b=b, stc=stc, ldx_mark=ldx_mark):
            k.wait(v, "vector", fG, ldx_mark)
            fV.inc(v.tensor_reduce(B["st"][:, stc:stc + 1], B["xt"][b][:, :],
                                   axis=AX.X, op=ALU.add))
        blk.vector(stats)
        sx_mark = fV.n

        def sumsq(s, b=b, stc=stc, ldx_mark=ldx_mark):
            k.wait(s, "scalar", fG, ldx_mark)
            k.wait(s, "scalar", fT, mk["lnf_free_T"])
            k.wait(s, "scalar", fT, mk["z2_free_T"])
            fS.inc(s.activation(B["lnf"][:, :], B["xt"][b][:, :], AF.Square,
                                accum_out=B["st"][:, stc + 1:stc + 2]))
        blk.scalar(sumsq)
        sxx_mark = fS.n

        def stats2(v, stc=stc, sx_mark=sx_mark, sxx_mark=sxx_mark):
            k.wait(v, "vector", fV, sx_mark)
            k.wait(v, "vector", fS, sxx_mark)
            fV.inc(v.tensor_single_scalar(B["st"][:, stc + 2:stc + 3],
                                          B["st"][:, stc:stc + 1],
                                          -1.0 / 2048.0, op=ALU.mult))
            k.wait(v, "vector", fV)
            fV.inc(v.tensor_tensor(B["st"][:, stc + 3:stc + 4],
                                   B["st"][:, stc + 2:stc + 3],
                                   B["st"][:, stc + 2:stc + 3], op=ALU.mult))
            k.wait(v, "vector", fV)
            fV.inc(v.tensor_scalar(B["st"][:, stc + 4:stc + 5],
                                   B["st"][:, stc + 1:stc + 2],
                                   1.0 / 2048.0, None, op0=ALU.mult))
            k.wait(v, "vector", fV)
            fV.inc(v.tensor_tensor(B["st"][:, stc + 4:stc + 5],
                                   B["st"][:, stc + 4:stc + 5],
                                   B["st"][:, stc + 3:stc + 4], op=ALU.subtract))
        blk.vector(stats2)
        var_mark = fV.n

        def rstd(s, stc=stc, var_mark=var_mark):
            k.wait(s, "scalar", fV, var_mark)
            fS.inc(s.activation(B["st"][:, stc + 5:stc + 6],
                                B["st"][:, stc + 4:stc + 5], AF.Ln, bias=k.c["mgc"][:, 2:3]))
            k.wait(s, "scalar", fS)
            fS.inc(s.activation(B["st"][:, stc + 5:stc + 6],
                                B["st"][:, stc + 5:stc + 6], AF.Exp, scale=-0.5))
        blk.scalar(rstd)
        rstd_mark = fS.n

        def lnapply(v, b=b, stc=stc, rstd_mark=rstd_mark):
            k.wait(v, "vector", fS, rstd_mark)
            fV.inc(v.tensor_scalar(B["xt"][b][:, :], B["xt"][b][:, :],
                                   B["st"][:, stc + 2:stc + 3],
                                   B["st"][:, stc + 5:stc + 6],
                                   op0=ALU.add, op1=ALU.mult))
        blk.vector(lnapply)
        ln_mark = fV.n

        lnf_marks = []
        for g4 in range(4):
            def trs(t, b=b, g4=g4, ln_mark=ln_mark):
                k.wait(t, "tensor", fV, ln_mark)
                k.wait(t, "tensor", fV, mk["pp5_free_V"])
                if lnf_marks:
                    k.wait(t, "tensor", fV, lnf_marks[-1])
                k.wait(t, "tensor", fS, mk["lnf_sq_S"])
                mm = None
                for q in range(4):
                    j = g4 * 4 + q
                    mm = t.matmul(pp[5][:, q * 128:(q + 1) * 128],
                                  B["xt"][b][:, j * 128:(j + 1) * 128],
                                  c_["I128f"][:, :], start=(q == 0),
                                  stop=(q == 3), is_transpose=True)
                fT.inc(mm)
            mk["lnf_sq_S"] = sxx_mark
            blk.tensor(trs)
            tr_mark = fT.n
            if g4 == 3:
                mk["xt_free"][b] = fT.n

            def evl(v, g4=g4, tr_mark=tr_mark):
                k.wait(v, "vector", fT, tr_mark)
                fV.inc(v.tensor_copy(B["lnf"][:, g4 * 512:(g4 + 1) * 512],
                                     pp[5][:, :]))
            blk.vector(evl)
            lnf_marks.append(fV.n)
            mk["pp5_free_V"] = fV.n

        z2_marks = []
        for g4 in range(4):
            def pa(t, g4=g4):
                k.wait(t, "tensor", fV, lnf_marks[-1])
                k.wait(t, "tensor", fV, mk["pp5_free_V"])
                k.wait(t, "tensor", fT, mk["z2_free_T"])
                if z2_marks:
                    k.wait(t, "tensor", fV, z2_marks[-1])
                mm = None
                for q in range(4):
                    a = g4 * 4 + q
                    mm = t.matmul(pp[5][:, q * 128:(q + 1) * 128],
                                  c_["H128f"][:, :],
                                  B["lnf"][:, a * 128:(a + 1) * 128],
                                  start=(q == 0), stop=(q == 3))
                fT.inc(mm)
            blk.tensor(pa)
            pa_mark = fT.n
            mk["lnf_free_T"] = fT.n

            def evz(v, g4=g4, pa_mark=pa_mark):
                k.wait(v, "vector", fT, pa_mark)
                src = AP(pp[5], 0, [[512, 128], [128, 4], [8, 16], [1, 8]])
                dst = AP(B["z2"], (g4 * 4) * 8,
                         [[2048, 128], [8, 4], [128, 16], [1, 8]])
                fV.inc(v.tensor_copy(dst, src))
            blk.vector(evz)
            z2_marks.append(fV.n)
            mk["pp5_free_V"] = fV.n

        u2_marks = []
        red_marks = []
        for gq in range(4):
            def ftr(t, gq=gq):
                k.wait(t, "tensor", fV, z2_marks[-1])
                if u2_marks:
                    k.wait(t, "tensor", fV, u2_marks[-1])
                mm = None
                for q in range(4):
                    g = gq * 4 + q
                    mm = t.matmul(pp[6][:, q * 128:(q + 1) * 128],
                                  B["z2"][:, g * 128:(g + 1) * 128],
                                  c_["I128f"][:, :], start=(q == 0),
                                  stop=(q == 3), is_transpose=True)
                fT.inc(mm)
            blk.tensor(ftr)
            ftr_mark = fT.n

            def evu(v, gq=gq, ftr_mark=ftr_mark):
                k.wait(v, "vector", fT, ftr_mark)
                fV.inc(v.tensor_copy(B["z2"][:, gq * 512:(gq + 1) * 512],
                                     pp[6][:, :]))
            blk.vector(evu)
            u2_marks.append(fV.n)

            def m1(t, gq=gq):
                k.wait(t, "tensor", fV, u2_marks[-1])
                if red_marks:
                    k.wait(t, "tensor", fV, red_marks[-1])
                mm = None
                for q in range(4):
                    g = gq * 4 + q
                    mm = t.matmul(pp[7][:, q * 128:(q + 1) * 128],
                                  B["z2"][:, g * 128:(g + 1) * 128],
                                  c_["Bmax2048"][:, :], start=(q == 0),
                                  stop=(q == 3))
                fT.inc(mm)
            blk.tensor(m1)
            m1_mark = fT.n

            def red(v, gq=gq, m1_mark=m1_mark):
                k.wait(v, "vector", fT, m1_mark)
                src = AP(pp[7], 0, [[512, 128], [16, 32], [1, 16]])
                fV.inc(v.tensor_reduce(B["lnf"][:, gq * 32:(gq + 1) * 32],
                                       src, axis=AX.X, op=ALU.max,
                                       apply_absolute_value=True))
            blk.vector(red)
            red_marks.append(fV.n)

        def xmax(t):
            k.wait(t, "tensor", fV, red_marks[-1])
            fT.inc(t.matmul(pp[7][:, 0:128], B["lnf"][:, 0:128],
                            c_["I128f"][:, :], start=True, stop=True,
                            is_transpose=True))
        blk.tensor(xmax)
        xm_mark = fT.n

        def sz2chain(v, stc=stc, xm_mark=xm_mark):
            k.wait(v, "vector", fT, xm_mark)
            fV.inc(v.tensor_reduce(B["st"][:, stc + 6:stc + 7], pp[7][:, 0:128],
                                   axis=AX.X, op=ALU.max))
            k.wait(v, "vector", fV)
            fV.inc(v.tensor_scalar(B["st"][:, stc + 7:stc + 8],
                                   B["st"][:, stc + 6:stc + 7],
                                   SQ2 * EPS, None, op0=ALU.max))
            k.wait(v, "vector", fV)
            fV.inc(v.reciprocal(B["st"][:, stc + 8:stc + 9],
                                B["st"][:, stc + 7:stc + 8]))
            k.wait(v, "vector", fV)
            fV.inc(v.tensor_single_scalar(B["st"][:, stc + 8:stc + 9],
                                          B["st"][:, stc + 8:stc + 9],
                                          127.0, op=ALU.mult))
            k.wait(v, "vector", fV)
            fV.inc(v.tensor_single_scalar(B["st"][:, stc + 9:stc + 10],
                                          B["st"][:, stc + 7:stc + 8],
                                          float(np.float32(1.0 / (127.0 * SQ2))),
                                          op=ALU.mult))
            k.wait(v, "vector", fV)
            fV.inc(v.tensor_scalar(B["lnf"][:, 256:272], c_["E2048"][:, :],
                                   B["st"][:, stc + 8:stc + 9], None,
                                   op0=ALU.mult))
            k.wait(v, "vector", fV)
            fV.inc(v.tensor_scalar(B["lnf"][:, 128:256], c_["I128f"][:, :],
                                   B["st"][:, stc + 9:stc + 10], None,
                                   op0=ALU.mult))
        blk.vector(sz2chain)
        sub_mark = fV.n

        def summ(t, sub_mark=sub_mark):
            k.wait(t, "tensor", fV, sub_mark)
            t.matmul(pp[7][:, 384:400], c_["P2048"][:, :],
                     B["lnf"][:, 256:272], start=True, stop=True)
            fT.inc(t.matmul(pp[7][:, 128:256], c_["ones128f"][:, :],
                            B["lnf"][:, 128:256], start=True, stop=True))
        blk.tensor(summ)
        summ_mark = fT.n

        def suev(v, tsub=tsub, summ_mark=summ_mark):
            k.wait(v, "vector", fT, summ_mark)
            v.tensor_copy(B["lnf"][:, 288:304], pp[7][:, 384:400])
            fV.inc(v.tensor_copy(B["abc"][:, tsub * 128:(tsub + 1) * 128],
                                 pp[7][:, 128:256]))
        blk.vector(suev)
        suev_mark = fV.n

        def uscale(v, suev_mark=suev_mark):
            k.wait(v, "vector", fV, suev_mark)
            mmv = None
            for g in range(16):
                mmv = v.tensor_scalar(B["z2"][:, g * 128:(g + 1) * 128],
                                      B["z2"][:, g * 128:(g + 1) * 128],
                                      B["lnf"][:, 288 + g:289 + g], None,
                                      op0=ALU.mult)
            fV.inc(mmv)
        blk.vector(uscale)
        us_mark = fV.n

        ev2_marks = []
        for gq in range(4):
            def m2(t, gq=gq, us_mark=us_mark):
                k.wait(t, "tensor", fV, us_mark)
                k.wait(t, "tensor", fT, mk["xq_free_T"])
                if ev2_marks:
                    k.wait(t, "tensor", fS, ev2_marks[-1])
                mm = None
                for q in range(4):
                    g = gq * 4 + q
                    t.matmul(pp[6][:, q * 128:(q + 1) * 128],
                             B["z2"][:, g * 128:(g + 1) * 128],
                             c_["Bev2048"][:, :], start=True, stop=False)
                    mm = t.matmul(pp[6][:, q * 128:(q + 1) * 128],
                                  c_["ones1"][:1, :], c_["Mrow"][:1, :],
                                  start=False, stop=True)
                fT.inc(mm)
            blk.tensor(m2)
            m2_mark = fT.n
            mk["z2_free_T"] = fT.n

            def evq(s, gq=gq, tsub=tsub, m2_mark=m2_mark):
                k.wait(s, "scalar", fT, m2_mark)
                src = AP(pp[6], 0, [[512, 128], [128, 4], [8, 16], [1, 8]])
                dst = AP(B["xq"], tsub * 128 + gq * 4 * 8,
                         [[8192, 128], [8, 4], [512, 16], [1, 8]])
                fS.inc(s.activation(dst, src, AF.Identity,
                                    bias=k.c["mgc"][:, 1:2]))
            blk.scalar(evq)
            ev2_marks.append(fS.n)
        xq_evac_marks.append(fS.n)

    def xqscale(v, ms=tuple(xq_evac_marks)):
        for m in ms:
            k.wait(v, "vector", fS, m)
        mmv = None
        for a in range(16):
            mmv = v.tensor_tensor(B["xq"][:, a * 512:(a + 1) * 512],
                                  B["xq"][:, a * 512:(a + 1) * 512],
                                  B["abc"][:, :], op=ALU.mult)
        fV.inc(mmv)
    blk.vector(xqscale)
    mk["xq_ready_V"] = fV.n


def _ffn_stageBC(k, blk, B, scr, wsid, sems, mk, c):
    c_ = k.c
    fW, fG, fT, fV, fS = (sems[x] for x in ("fW", "fG", "fT", "fV", "fS"))
    pp = B["pp"]
    SQ8 = float(np.float32(math.sqrt(8192.0)))

    for s in range(64):
        wb = s % 2

        def ldw(sy, s=s, wb=wb):
            k.wait(sy, "sync", fT, mk["wg_free"][wb])
            src = AP(scr["g"], s * 16 * 128 * 128,
                     [[128, 128], [128 * 128, 16], [1, 128]])
            fW.dma(sy.dma_start(B["wg"][wb][:, :], src))
            k.wait(sy, "sync", fT, mk["wu_free"][wb])
            src = AP(scr["u"], s * 16 * 128 * 128,
                     [[128, 128], [128 * 128, 16], [1, 128]])
            fW.dma(sy.dma_start(B["wu"][wb][:, :], src))
        blk.sync(ldw)
        ldw_mark = fW.n

        def gup(t, s=s, wb=wb, ldw_mark=ldw_mark):
            k.wait(t, "tensor", fV, mk["xq_ready_V"])
            k.wait(t, "tensor", fW, ldw_mark)
            k.wait(t, "tensor", fT, mk["z8_free_T"])
            k.wait(t, "tensor", fS, mk["silu_done_S"])
            k.wait(t, "tensor", fV, mk["actm_done_V"])
            mm = None
            for hc in range(16):
                mm = t.matmul(pp[0][:, :], B["wg"][wb][:, hc * 128:(hc + 1) * 128],
                              B["xq"][:, hc * 512:(hc + 1) * 512],
                              start=(hc == 0), stop=(hc == 15))
            mk["wg_free"][wb] = fT.n + 1
            fT.inc(mm)
            for hc in range(16):
                mm = t.matmul(pp[1][:, :], B["wu"][wb][:, hc * 128:(hc + 1) * 128],
                              B["xq"][:, hc * 512:(hc + 1) * 512],
                              start=(hc == 0), stop=(hc == 15))
            mk["wu_free"][wb] = fT.n + 1
            fT.inc(mm)
        blk.tensor(gup)
        gup_mark = fT.n
        if s == 63:
            mk["xq_free_T"] = fT.n

        def silu(s_, gup_mark=gup_mark):
            k.wait(s_, "scalar", fT, gup_mark)
            k.wait(s_, "scalar", fV, mk["actm_done_V"])
            fS.inc(s_.activation(B["sg"][:, :], pp[0][:, :], AF.Silu,
                                 scale=wsid[:, 0:1]))
        blk.scalar(silu)
        mk["silu_done_S"] = fS.n
        silu_mark = fS.n

        def actm(v, gup_mark=gup_mark, silu_mark=silu_mark):
            k.wait(v, "vector", fT, gup_mark)
            k.wait(v, "vector", fS, silu_mark)
            k.wait(v, "vector", fT, mk["pa8_done_T"])
            fV.inc(v.scalar_tensor_tensor(B["sg"][:, :], pp[1][:, :],
                                          wsid[:, 2:3], B["sg"][:, :],
                                          op0=ALU.mult, op1=ALU.mult))
        blk.vector(actm)
        mk["actm_done_V"] = fV.n
        actm_mark = fV.n

        def pa8(t, actm_mark=actm_mark):
            k.wait(t, "tensor", fV, actm_mark)
            k.wait(t, "tensor", fV, mk["z8ev_done_V"])
            fT.inc(t.matmul(pp[2][:, :], c_["H128f"][:, :], B["sg"][:, :],
                            start=True, stop=True))
        blk.tensor(pa8)
        mk["pa8_done_T"] = fT.n
        pa8_mark = fT.n

        def evz8(v, s=s, pa8_mark=pa8_mark):
            k.wait(v, "vector", fT, pa8_mark)
            src = AP(pp[2], 0, [[512, 128], [2, 256], [1, 2]])
            dst = AP(B["z8"], s * 2, [[32768, 128], [128, 256], [1, 2]])
            fV.inc(v.tensor_copy(dst, src))
        blk.vector(evz8)
        mk["z8ev_done_V"] = fV.n
    z8_ready_V = fV.n

    u8_marks = []
    red8_marks = []
    for gq in range(64):
        def ftr8(t, gq=gq, z8r=z8_ready_V):
            k.wait(t, "tensor", fV, z8r)
            if u8_marks:
                k.wait(t, "tensor", fV, u8_marks[-1])
            mm = None
            for q in range(4):
                g = gq * 4 + q
                mm = t.matmul(pp[3][:, q * 128:(q + 1) * 128],
                              B["z8"][:, g * 128:(g + 1) * 128],
                              c_["I128f"][:, :], start=(q == 0), stop=(q == 3),
                              is_transpose=True)
            fT.inc(mm)
        blk.tensor(ftr8)
        ftr8_mark = fT.n

        def evu8(v, gq=gq, ftr8_mark=ftr8_mark):
            k.wait(v, "vector", fT, ftr8_mark)
            fV.inc(v.tensor_copy(B["z8"][:, gq * 512:(gq + 1) * 512], pp[3][:, :]))
        blk.vector(evu8)
        u8_marks.append(fV.n)

        def m18(t, gq=gq):
            k.wait(t, "tensor", fV, u8_marks[-1])
            if red8_marks:
                k.wait(t, "tensor", fV, red8_marks[-1])
            mm = None
            for q in range(4):
                g = gq * 4 + q
                mm = t.matmul(pp[4][:, q * 128:(q + 1) * 128],
                              B["z8"][:, g * 128:(g + 1) * 128],
                              c_["Bmax8192"][:, :], start=(q == 0), stop=(q == 3))
            fT.inc(mm)
        blk.tensor(m18)
        m18_mark = fT.n

        def red8(v, gq=gq, m18_mark=m18_mark):
            k.wait(v, "vector", fT, m18_mark)
            src = AP(pp[4], 0, [[512, 128], [64, 8], [1, 64]])
            fV.inc(v.tensor_reduce(B["mxp"][:, gq * 8:(gq + 1) * 8], src,
                                   axis=AX.X, op=ALU.max,
                                   apply_absolute_value=True))
        blk.vector(red8)
        red8_marks.append(fV.n)

    sz8_done = []
    for tsub in range(4):
        def xm8(t, tsub=tsub):
            k.wait(t, "tensor", fV, red8_marks[-1])
            if sz8_done:
                k.wait(t, "tensor", fV, sz8_done[-1])
            fT.inc(t.matmul(pp[3][:, 0:128],
                            B["mxp"][:, tsub * 128:(tsub + 1) * 128],
                            c_["I128f"][:, :], start=True, stop=True,
                            is_transpose=True))
        blk.tensor(xm8)
        xm8_mark = fT.n

        def sz8chain(v, tsub=tsub, xm8_mark=xm8_mark):
            sc = tsub * 4
            k.wait(v, "vector", fT, xm8_mark)
            fV.inc(v.tensor_reduce(B["sz8"][:, sc:sc + 1], pp[3][:, 0:128],
                                   axis=AX.X, op=ALU.max))
            k.wait(v, "vector", fV)
            fV.inc(v.tensor_scalar(B["sz8"][:, sc + 1:sc + 2],
                                   B["sz8"][:, sc:sc + 1], SQ8 * EPS, None,
                                   op0=ALU.max))
            k.wait(v, "vector", fV)
            fV.inc(v.reciprocal(B["sz8"][:, sc + 2:sc + 3],
                                B["sz8"][:, sc + 1:sc + 2]))
            k.wait(v, "vector", fV)
            fV.inc(v.tensor_single_scalar(B["sz8"][:, sc + 2:sc + 3],
                                          B["sz8"][:, sc + 2:sc + 3], 127.0,
                                          op=ALU.mult))
            k.wait(v, "vector", fV)
            fV.inc(v.tensor_single_scalar(B["sz8"][:, sc + 3:sc + 4],
                                          B["sz8"][:, sc + 1:sc + 2],
                                          float(np.float32(1.0 / (127.0 * SQ8))),
                                          op=ALU.mult))
            k.wait(v, "vector", fV)
            fV.inc(v.tensor_tensor(B["wacol"][:, tsub:tsub + 1],
                                   B["sz8"][:, sc + 3:sc + 4], wsid[:, 4:5],
                                   op=ALU.mult))
            k.wait(v, "vector", fV)
            fV.inc(v.tensor_scalar(B["em"][:, :], c_["E8192"][:, :],
                                   B["sz8"][:, sc + 2:sc + 3], None,
                                   op0=ALU.mult))
        blk.vector(sz8chain)
        sz8_mark = fV.n

        def su8m(t, sz8_mark=sz8_mark):
            k.wait(t, "tensor", fV, sz8_mark)
            fT.inc(t.matmul(pp[3][:, 128:192], c_["P8192"][:, :], B["em"][:, :],
                            start=True, stop=True))
        blk.tensor(su8m)
        su8m_mark = fT.n

        def su8ev(v, tsub=tsub, su8m_mark=su8m_mark):
            k.wait(v, "vector", fT, su8m_mark)
            fV.inc(v.tensor_copy(B["su8"][:, tsub * 64:(tsub + 1) * 64],
                                 pp[3][:, 128:192]))
        blk.vector(su8ev)
        sz8_done.append(fV.n)
    su8_ready_V = fV.n

    def wperm(t, sur=su8_ready_V):
        k.wait(t, "tensor", fV, sur)
        fT.inc(t.matmul(pp[3][:, 192:196], c_["PermEO"][:, :], B["wacol"][:, :],
                        start=True, stop=True))
    blk.tensor(wperm)
    wperm_mark = fT.n

    def wpev(v, wperm_mark=wperm_mark):
        k.wait(v, "vector", fT, wperm_mark)
        fV.inc(v.tensor_copy(B["wacol"][:, :], pp[3][:, 192:196]))
    blk.vector(wpev)
    mk["wacol_V"] = fV.n

    yq_marks = []
    for gq in range(64):
        def us8(v, gq=gq, sur=su8_ready_V):
            k.wait(v, "vector", fV, sur)
            mmv = None
            for q in range(4):
                g = gq * 4 + q
                mmv = v.tensor_scalar(B["z8"][:, g * 128:(g + 1) * 128],
                                      B["z8"][:, g * 128:(g + 1) * 128],
                                      B["su8"][:, g:g + 1], None, op0=ALU.mult)
            fV.inc(mmv)
        blk.vector(us8)
        us8_mark = fV.n

        def m28(t, gq=gq, us8_mark=us8_mark):
            k.wait(t, "tensor", fV, us8_mark)
            if yq_marks:
                k.wait(t, "tensor", fS, yq_marks[-1])
            mm = None
            for q in range(4):
                g = gq * 4 + q
                t.matmul(pp[4][:, q * 128:(q + 1) * 128],
                         B["z8"][:, g * 128:(g + 1) * 128],
                         c_["Bev8192"][:, :], start=True, stop=False)
                mm = t.matmul(pp[4][:, q * 128:(q + 1) * 128],
                              c_["ones1"][:1, :], c_["Mrow"][:1, :],
                              start=False, stop=True)
            fT.inc(mm)
        blk.tensor(m28)
        m28_mark = fT.n

        def evy(s, gq=gq, m28_mark=m28_mark):
            k.wait(s, "scalar", fT, m28_mark)
            src = AP(pp[4], 0, [[512, 128], [1, 512]])
            dst = AP(B["yq"], gq * 512, [[65536, 128], [1, 512]])
            fS.inc(s.activation(dst, src, AF.Identity, bias=k.c["mgc"][:, 1:2]))
        blk.scalar(evy)
        yq_marks.append(fS.n)
    mk["yq_ready_S"] = fS.n


def _ffn_stageD(k, blk, B, x_in, x_out, scr, wsid, sems, mk, c):
    fW, fG, fT, fV, fS = (sems[x] for x in ("fW", "fG", "fT", "fV", "fS"))
    pp = B["pp"]
    for tqp in range(2):
        for ocp in range(2):
            for ap_ in range(64):
                wb = ap_ % 2

                def ldwd(sy, ap_=ap_, wb=wb, ocp=ocp):
                    k.wait(sy, "sync", fT, mk["wd_free"][wb])
                    src = AP(scr["d"], (ap_ * 16 + ocp * 8) * 128 * 128,
                             [[128, 128], [128 * 128, 8], [1, 128]])
                    fW.dma(sy.dma_start(B["wd"][wb][:, :], src))
                blk.sync(ldwd)
                ldwd_mark = fW.n

                def dn(t, ap_=ap_, wb=wb, tqp=tqp, ldwd_mark=ldwd_mark):
                    k.wait(t, "tensor", fS, mk["yq_ready_S"])
                    k.wait(t, "tensor", fW, ldwd_mark)
                    if ap_ == 0:
                        k.wait(t, "tensor", fV, mk["dn_ev_V"])
                    mms = []
                    for tq in range(2):
                        tqg = tqp * 2 + tq
                        for oc in range(2):
                            pb = pp[tq * 2 + oc]
                            for r in range(2):
                                lhs = AP(B["yq"], tqg * 64 * 128 + ap_ * 2 + r,
                                         [[65536, 128], [128, 64]])
                                mms.append(t.matmul(
                                    pb[r * 64:(r + 1) * 64, :], lhs,
                                    B["wd"][wb][:, oc * 512:(oc + 1) * 512],
                                    start=(ap_ == 0), stop=(ap_ == 63)))
                    for m_ in mms:
                        fT.inc(m_)
                    mk["wd_free"][wb] = fT.n
                    mk["dn_last_T"] = fT.n
                blk.tensor(dn)
            dn_mark = fT.n
            if tqp == 1 and ocp == 1:
                mk["z8_free_T"] = fT.n

            for tq in range(2):
                for oc in range(2):
                    tqg = tqp * 2 + tq
                    trow = c * 512 + tqg * 128
                    colo = ocp * 1024 + oc * 512

                    def ldres(g, trow=trow, colo=colo):
                        k.wait(g, "gpsimd", fV, mk["xres_free_V"])
                        fG.dma(g.dma_start(
                            B["xres"][:, :],
                            x_in[trow:trow + 128, colo:colo + 512]))
                    blk.gpsimd(ldres)
                    ldres_mark = fG.n

                    def dnsc(v, tq=tq, oc=oc, tqg=tqg, dn_mark=dn_mark):
                        k.wait(v, "vector", fT, dn_mark)
                        k.wait(v, "vector", fV, mk.get("wacol_V", 0))
                        k.wait(v, "vector", fT, mk.get("xo2_perm_T", 0))
                        k.wait(v, "vector", fG, mk["xo_free_G"])
                        fV.inc(v.tensor_scalar(
                            B["xo"][:, :], pp[tq * 2 + oc][:, :],
                            B["wacol"][:, tqg:tqg + 1], None, op0=ALU.mult))
                        mk["dn_ev_V"] = fV.n
                    blk.vector(dnsc)
                    dnsc_mark = fV.n

                    def dnperm(t, dnsc_mark=dnsc_mark):
                        k.wait(t, "tensor", fV, dnsc_mark)
                        k.wait(t, "tensor", fV, mk.get("xo_fin_V", 0))
                        fT.inc(t.matmul(pp[4][:, :], k.c["PermEOT"][:, :],
                                        B["xo"][:, :], start=True, stop=True))
                        mk["xo2_perm_T"] = fT.n
                    blk.tensor(dnperm)
                    dnperm_mark = fT.n

                    def dnfin(v, dnperm_mark=dnperm_mark, ldres_mark=ldres_mark):
                        k.wait(v, "vector", fT, dnperm_mark)
                        k.wait(v, "vector", fG, ldres_mark)
                        k.wait(v, "vector", fG, mk["xo_free_G"])
                        fV.inc(v.scalar_tensor_tensor(
                            B["xo"][:, :], pp[4][:, :], 1.0, B["xres"][:, :],
                            op0=ALU.mult, op1=ALU.add))
                        mk["xres_free_V"] = fV.n
                        mk["xo_fin_V"] = fV.n
                    blk.vector(dnfin)
                    dnfin_mark = fV.n

                    def stx(g, trow=trow, colo=colo, dnfin_mark=dnfin_mark):
                        k.wait(g, "gpsimd", fV, dnfin_mark)
                        fG.dma(g.dma_start(
                            x_out[trow:trow + 128, colo:colo + 512],
                            B["xo"][:, :]))
                        mk["xo_free_G"] = fG.n
                    blk.gpsimd(stx)


def emit_ffn_app(k, blk, B, x_in, x_out, scr, wsid, sems):
    mk = sems.setdefault("marks", {})
    for key, v0 in (("z8_free_T", 0), ("xq_free_T", 0), ("z2_free_T", 0),
                    ("lnf_free_T", 0), ("lnf_sq_S", 0), ("pp5_free_V", 0),
                    ("silu_done_S", 0), ("actm_done_V", 0), ("pa8_done_T", 0),
                    ("z8ev_done_V", 0), ("dn_ev_V", 0), ("xres_free_V", 0),
                    ("xo_free_G", 0), ("xq_ready_V", 0), ("yq_ready_S", 0)):
        mk.setdefault(key, v0)
    for key in ("xt_free", "wg_free", "wu_free"):
        mk.setdefault(key, [0, 0])
    mk.setdefault("wd_free", [0, 0])

    _ffn_stageA(k, blk, B, x_in, sems, mk, 0)
    for c in range(NCH):
        _ffn_stageBC(k, blk, B, scr, wsid, sems, mk, c)
        if c + 1 < NCH:
            _ffn_stageA(k, blk, B, x_in, sems, mk, c + 1)
        _ffn_stageD(k, blk, B, x_in, x_out, scr, wsid, sems, mk, c)
    return sems["fG"].n



def emit_attn_a(k, blk, x_in, x_out, W, sems):
    c_ = k.c
    nc = k.nc
    aW, aG, aT, aV, aS = (sems[x] for x in ("aW", "aG", "aT", "aV", "aS"))
    mk = {}
    es2 = ExitStack()
    lsb = lambda n, sh, dt: es2.enter_context(nc.sbuf_tensor(n, list(sh), dt))
    lps = lambda n, sh, dt=F32: es2.enter_context(nc.psum_tensor(n, list(sh), dt))

    xt = [lsb(f"a_xt{i}", (128, 2048), F32) for i in range(2)]
    ajk = lsb("a_jk", (128, 2048), F32)
    xres = [lsb(f"a_xres{i}", (128, 2048), F32) for i in range(2)]
    st = lsb("a_st", (128, 64), F32)
    lnb = lsb("a_lnb", (128, 2048), BF16)
    lnf8 = lsb("a_lnf8", (128, 8192), BF16)
    wcb = lsb("a_wcb", (128, 16 * 1024), BF16)
    wpb = lsb("a_wpb", (128, 8 * 1024), BF16)
    wdb = lsb("a_wdb", (128, 8 * 2048), BF16)
    cwb = lsb("a_cwb", (128, 8 * 512), BF16)
    qwb = lsb("a_qwb", (128, 8 * 512), BF16)
    kwb = lsb("a_kwb", (128, 8 * 512), BF16)
    vwb = lsb("a_vwb", (128, 4 * 1024), BF16)
    ctxw = lsb("a_ctxw", (128, 4 * 1024), BF16)
    probs = lsb("a_probs", (128, 512), BF16)
    pt = lsb("a_pt", (128, 512), BF16)
    biasT = lsb("a_biasT", (128, 4 * 512), BF16)
    bst = lsb("a_bst", (8, 384), F32)
    bcol = lsb("a_bcol", (128, 24), F32)
    bvrow = lsb("a_bvrow", (1, 1024), BF16)
    bdrow = lsb("a_bdrow", (1, 2048), BF16)
    sv = lsb("a_sv", (1, 257), F32)
    svpad = lsb("a_svpad", (1, 1023), F32)
    lcol = lsb("a_lcol", (128, 200), F32)
    pr0 = lps("a_pr0", (128, 512))
    pr1 = lps("a_pr1", (128, 512))
    ps_s = lps("a_pss", (128, 512))
    ps_t0 = lps("a_pst0", (128, 512), BF16)
    ps_t1 = lps("a_pst1", (128, 512), BF16)
    ps_c = lps("a_psc", (128, 64))
    ps_o = [lps(f"a_pso{i}", (128, 512)) for i in range(2)]

    def relb1(g):
        aG.dma(g.dma_start(sv[:1, :], W["relb"][None, :]))
    blk.gpsimd(relb1)
    relb_ld = aG.n

    def relb2(v, relb_ld=relb_ld):
        k.wait(v, "vector", aG, relb_ld)
        aV.inc(v.tensor_copy(svpad[:1, 383:640], sv[:1, :]))
        v.memset(svpad[:1, 0:383], 0.0)
        aV.inc(v.memset(svpad[:1, 640:1023], 0.0))
        k.wait(v, "vector", aV)
        aV.inc(v.tensor_scalar(svpad[:1, 0:383], svpad[:1, 0:383],
                               sv[:1, 0:1], None, op0=ALU.add))
        k.wait(v, "vector", aV)
        aV.inc(v.tensor_scalar(svpad[:1, 640:1023], svpad[:1, 640:1023],
                               sv[:1, 256:257], None, op0=ALU.add))
    blk.vector(relb2)
    svp_mark = aV.n

    def relb3(g, svp_mark=svp_mark):
        k.wait(g, "gpsimd", aV, svp_mark)
        aG.dma(g.dma_start(W["vd"], svpad[:1, :]))
        k.wait(g, "gpsimd", aG)
        for qc in range(4):
            srcap = AP(W["vd"], 384 - 128 * qc, [[1, 128], [1, 512]])
            aG.dma(g.dma_start(xt[0][:, qc * 512:(qc + 1) * 512], srcap))
    blk.gpsimd(relb3)
    vw_ld = aG.n

    for qc in range(4):
        pb = [pr0, pr1][qc % 2]

        def jmm(t, qc=qc, pb=pb, vw_ld=vw_ld):
            k.wait(t, "tensor", aG, vw_ld)
            if ("jpb", qc % 2) in mk:
                k.wait(t, "tensor", aV, mk[("jpb", qc % 2)])
            aT.inc(t.matmul(pb[:, :], c_["J128f"][:, :],
                            xt[0][:, qc * 512:(qc + 1) * 512],
                            start=True, stop=True))
        blk.tensor(jmm)
        jmm_mark = aT.n

        def jev(v, qc=qc, pb=pb, jmm_mark=jmm_mark):
            k.wait(v, "vector", aT, jmm_mark)
            aV.inc(v.tensor_copy(biasT[:, qc * 512:(qc + 1) * 512], pb[:, :]))
            mk[("jpb", qc % 2)] = aV.n
        blk.vector(jev)
    bias_ready = aV.n
    mk[("xt_free", 0)] = bias_ready
    xt0_bias_T = aT.n

    def ldwc(g):
        src = AP(W["wc"], 0, [[1024, 128], [128 * 1024, 16], [1, 1024]])
        aG.dma(g.dma_start(wcb[:, :], src))
        src = AP(W["wd"], 0, [[2048, 128], [128 * 2048, 8], [1, 2048]])
        aG.dma(g.dma_start(wdb[:, :], src))
        aG.dma(g.dma_start(bvrow[:1, :], W["bv"][None, :]))
        aG.dma(g.dma_start(bdrow[:1, :], W["bd"][None, :]))
        for i, nm in enumerate(("bc", "bq", "bk")):
            src = AP(W[nm], 0, [[128, 8], [1, 128]])
            aG.dma(g.dma_start(bst[0:8, i * 128:(i + 1) * 128], src))
    blk.gpsimd(ldwc)
    wc_ready = aG.n

    def bcolmm(t, wc_ready=wc_ready):
        k.wait(t, "tensor", aG, wc_ready)
        k.wait(t, "tensor", aV, mk[("jpb", 0)])
        mm = None
        for i in range(3):
            mm = t.matmul(pr0[:, i * 8:(i + 1) * 8], bst[0:8, i * 128:(i + 1) * 128],
                          c_["I128f"][0:8, 0:8], start=True, stop=True,
                          is_transpose=True)
        aT.inc(mm)
    blk.tensor(bcolmm)
    bcm = aT.n

    def bcolev(v, bcm=bcm):
        k.wait(v, "vector", aT, bcm)
        aV.inc(v.tensor_copy(bcol[:, 0:24], pr0[:, 0:24]))
        k.wait(v, "vector", aV)
        aV.inc(v.tensor_scalar(bcol[:, 8:16], bcol[:, 8:16], 0.125, None,
                               op0=ALU.mult))
    blk.vector(bcolev)
    bcol_ready = aV.n

    for w in range(4):
        for tt in range(4):
            b = tt % 2
            trow = w * 512 + tt * 128
            stc = tt * 8

            def ldx(sy, b=b, trow=trow, w=w, tt=tt):
                if (w, tt) == (0, 0):
                    k.wait(sy, "sync", aT, xt0_bias_T)
                k.wait(sy, "sync", aV, mk.get(("xt_free", b), 0))
                aW.dma(sy.dma_start(xt[b][:, :], x_in[trow:trow + 128, :]))
            blk.sync(ldx)
            ldm = aW.n

            def stats(v, b=b, stc=stc, ldm=ldm):
                k.wait(v, "vector", aW, ldm)
                aV.inc(v.tensor_reduce(st[:, stc:stc + 1], xt[b][:, :],
                                       axis=AX.X, op=ALU.add))
            blk.vector(stats)
            sxm = aV.n

            def sumsq(s, b=b, stc=stc, ldm=ldm):
                k.wait(s, "scalar", aW, ldm)
                aS.inc(s.activation(ajk[:, :], xt[b][:, :], AF.Square,
                                    accum_out=st[:, stc + 1:stc + 2]))
            blk.scalar(sumsq)
            sqm = aS.n

            def stats2(v, stc=stc, sxm=sxm, sqm=sqm):
                k.wait(v, "vector", aV, sxm)
                k.wait(v, "vector", aS, sqm)
                aV.inc(v.tensor_single_scalar(st[:, stc + 2:stc + 3],
                                              st[:, stc:stc + 1],
                                              -1.0 / 2048.0, op=ALU.mult))
                k.wait(v, "vector", aV)
                aV.inc(v.tensor_tensor(st[:, stc + 3:stc + 4],
                                       st[:, stc + 2:stc + 3],
                                       st[:, stc + 2:stc + 3], op=ALU.mult))
                k.wait(v, "vector", aV)
                aV.inc(v.tensor_scalar(st[:, stc + 4:stc + 5],
                                       st[:, stc + 1:stc + 2],
                                       1.0 / 2048.0, None, op0=ALU.mult))
                k.wait(v, "vector", aV)
                aV.inc(v.tensor_tensor(st[:, stc + 4:stc + 5],
                                       st[:, stc + 4:stc + 5],
                                       st[:, stc + 3:stc + 4], op=ALU.subtract))
            blk.vector(stats2)
            varm = aV.n

            def rstd(s, stc=stc, varm=varm):
                k.wait(s, "scalar", aV, varm)
                aS.inc(s.activation(st[:, stc + 5:stc + 6],
                                    st[:, stc + 4:stc + 5], AF.Ln,
                                    bias=c_["mgc"][:, 2:3]))
                k.wait(s, "scalar", aS)
                aS.inc(s.activation(st[:, stc + 5:stc + 6],
                                    st[:, stc + 5:stc + 6], AF.Exp, scale=-0.5))
            blk.scalar(rstd)
            rsm = aS.n

            def lnapply(v, b=b, stc=stc, rsm=rsm):
                k.wait(v, "vector", aS, rsm)
                k.wait(v, "vector", aT, mk.get("lnb_free_T", 0))
                aV.inc(v.tensor_scalar(lnb[:, :], xt[b][:, :],
                                       st[:, stc + 2:stc + 3],
                                       st[:, stc + 5:stc + 6],
                                       op0=ALU.add, op1=ALU.mult))
                mk[("xt_free", b)] = aV.n
            blk.vector(lnapply)
            lam = aV.n

            for g4 in range(4):
                pb = [ps_t0, ps_t1][g4 % 2]

                def trs(t, g4=g4, pb=pb, lam=lam, w=w):
                    k.wait(t, "tensor", aV, lam)
                    k.wait(t, "tensor", aV, mk.get(("ptb", g4 % 2), 0))
                    if g4 == 0 and w > 0:
                        k.wait(t, "tensor", aT, mk.get("lnf8_free_T", 0))
                    mm = None
                    for q2 in range(4):
                        a = g4 * 4 + q2
                        mm = t.matmul(pb[:, q2 * 128:(q2 + 1) * 128],
                                      lnb[:, a * 128:(a + 1) * 128],
                                      c_["I128b"][:, :], start=(q2 == 0),
                                      stop=(q2 == 3), is_transpose=True)
                    aT.inc(mm)
                    if g4 == 3:
                        mk["lnb_free_T"] = aT.n
                blk.tensor(trs)
                trm = aT.n

                def evl(v, g4=g4, pb=pb, tt=tt, trm=trm):
                    k.wait(v, "vector", aT, trm)
                    src = AP(pb, 0, [[512, 128], [128, 4], [1, 128]])
                    dst = AP(lnf8, (g4 * 4) * 512 + tt * 128,
                             [[8192, 128], [512, 4], [1, 128]])
                    aV.inc(v.tensor_copy(dst, src))
                    mk[("ptb", g4 % 2)] = aV.n
                blk.vector(evl)
        lnf_ready = aV.n

        for o in range(8):
            pb = [pr0, pr1][o % 2]

            def cmm(t, o=o, pb=pb, lnf_ready=lnf_ready, w=w):
                k.wait(t, "tensor", aV, lnf_ready)
                k.wait(t, "tensor", aG, wc_ready)
                k.wait(t, "tensor", aS, mk.get(("cpb", o % 2), 0))
                if o == 0 and w > 0:
                    k.wait(t, "tensor", aT, mk.get("cwb_free_T", 0))
                mm = None
                for ht in range(16):
                    mm = t.matmul(
                        pb[:, :],
                        wcb[:, ht * 1024 + o * 128:ht * 1024 + (o + 1) * 128],
                        lnf8[:, ht * 512:(ht + 1) * 512],
                        start=(ht == 0), stop=(ht == 15))
                aT.inc(mm)
                if o == 7:
                    mk["lnf8_free_T"] = aT.n
            blk.tensor(cmm)
            cm = aT.n

            def cev(s, o=o, pb=pb, cm=cm):
                k.wait(s, "scalar", aT, cm)
                k.wait(s, "scalar", aV, bcol_ready)
                aS.inc(s.activation(cwb[:, o * 512:(o + 1) * 512], pb[:, :],
                                    AF.Identity, bias=bcol[:, o:o + 1]))
                mk[("cpb", o % 2)] = aS.n
            blk.scalar(cev)
        c_ready = aS.n

        for pi, (proj, dstb, boff) in enumerate(
                (("wq", qwb, 8), ("wk", kwb, 16))):
            def ldw(g, proj=proj, w=w, pi=pi):
                if w > 0 or pi > 0:
                    k.wait(g, "gpsimd", aT, mk.get("wpb_free_T", 0))
                src = AP(W[proj], 0, [[1024, 128], [128 * 1024, 8], [1, 1024]])
                aG.dma(g.dma_start(wpb[:, :], src))
            blk.gpsimd(ldw)
            wpm = aG.n

            for o in range(8):
                pb = [pr0, pr1][o % 2]

                def pmm(t, o=o, pb=pb, wpm=wpm, c_ready=c_ready, w=w, pi=pi):
                    k.wait(t, "tensor", aS, c_ready)
                    k.wait(t, "tensor", aG, wpm)
                    k.wait(t, "tensor", aS, mk.get(("cpb", o % 2), 0))
                    if o == 0 and pi == 0 and w > 0:
                        k.wait(t, "tensor", aT, mk.get("qk_free_T", 0))
                    mm = None
                    for ht in range(8):
                        mm = t.matmul(
                            pb[:, :],
                            wpb[:, ht * 1024 + o * 128:ht * 1024 + (o + 1) * 128],
                            cwb[:, ht * 512:(ht + 1) * 512],
                            start=(ht == 0), stop=(ht == 7))
                    aT.inc(mm)
                    if o == 7:
                        mk["wpb_free_T"] = aT.n
                blk.tensor(pmm)
                pm = aT.n

                def pev(s, o=o, pb=pb, pm=pm, dstb=dstb, boff=boff, pi=pi):
                    k.wait(s, "scalar", aT, pm)
                    if pi == 0:
                        aS.inc(s.activation(dstb[:, o * 512:(o + 1) * 512],
                                            pb[:, :], AF.Identity,
                                            bias=bcol[:, boff + o:boff + o + 1],
                                            scale=c_["mgc"][:, 3:4]))
                    else:
                        aS.inc(s.activation(dstb[:, o * 512:(o + 1) * 512],
                                            pb[:, :], AF.Identity,
                                            bias=bcol[:, boff + o:boff + o + 1]))
                    mk[("cpb", o % 2)] = aS.n
                blk.scalar(pev)

        def ldwv(g):
            k.wait(g, "gpsimd", aT, mk.get("wpb_free_T", 0))
            src = AP(W["wv"], 0, [[1024, 128], [128 * 1024, 8], [1, 1024]])
            aG.dma(g.dma_start(wpb[:, :], src))
        blk.gpsimd(ldwv)
        wvm = aG.n

        for tt in range(4):
            for oh in range(2):
                i8 = tt * 2 + oh
                pb = [pr0, pr1][i8 % 2]

                def vmm(t, tt=tt, oh=oh, i8=i8, pb=pb, wvm=wvm,
                        c_ready=c_ready, w=w):
                    k.wait(t, "tensor", aS, c_ready)
                    k.wait(t, "tensor", aG, wvm)
                    k.wait(t, "tensor", aS, mk.get(("cpb", i8 % 2), 0))
                    if i8 == 0 and w > 0:
                        k.wait(t, "tensor", aT, mk.get("vwb_free_T", 0))
                    mm = None
                    for ht in range(8):
                        t.matmul(
                            pb[:, :],
                            cwb[:, ht * 512 + tt * 128:ht * 512 + (tt + 1) * 128],
                            wpb[:, ht * 1024 + oh * 512:ht * 1024 + (oh + 1) * 512],
                            start=(ht == 0), stop=False)
                    mm = t.matmul(pb[:, :], c_["ones1b"][:1, :],
                                  bvrow[:1, oh * 512:(oh + 1) * 512],
                                  start=False, stop=True)
                    aT.inc(mm)
                    if i8 == 7:
                        mk["wpb_free_T"] = aT.n
                        mk["cwb_free_T"] = aT.n
                blk.tensor(vmm)
                vm = aT.n

                def vev(s, tt=tt, oh=oh, i8=i8, pb=pb, vm=vm):
                    k.wait(s, "scalar", aT, vm)
                    aS.inc(s.activation(
                        vwb[:, tt * 1024 + oh * 512:tt * 1024 + (oh + 1) * 512],
                        pb[:, :], AF.Identity))
                    mk[("cpb", i8 % 2)] = aS.n
                blk.scalar(vev)
        v_ready = aS.n

        for h in range(16):
            hf = h // 2
            po = (h % 2) * 64
            for qc in range(4):
                iq = h * 4 + qc

                def smm(t, hf=hf, po=po, qc=qc, v_ready=v_ready):
                    k.wait(t, "tensor", aS, v_ready)
                    k.wait(t, "tensor", aV, bias_ready)
                    k.wait(t, "tensor", aS, mk.get("pss_free_S", 0))
                    t.matmul(ps_s[:, :],
                             qwb[po:po + 64,
                                 hf * 512 + qc * 128:hf * 512 + (qc + 1) * 128],
                             kwb[po:po + 64, hf * 512:(hf + 1) * 512],
                             start=True, stop=False)
                    aT.inc(t.matmul(ps_s[:, :], c_["I128b"][:, :],
                                    biasT[:, qc * 512:(qc + 1) * 512],
                                    start=False, stop=True))
                    if h == 15 and qc == 3:
                        mk["qk_free_T"] = aT.n
                blk.tensor(smm)
                sm = aT.n

                def smax(v, h=h, qc=qc, sm=sm):
                    k.wait(v, "vector", aT, sm)
                    aV.inc(v.tensor_reduce(lcol[:, 128 + qc:129 + qc],
                                           ps_s[:, :], axis=AX.X, op=ALU.max,
                                           negate=True))
                blk.vector(smax)
                sxm2 = aV.n

                def sexp(s, h=h, qc=qc, iq=iq, sxm2=sxm2, w=w):
                    k.wait(s, "scalar", aV, sxm2)
                    k.wait(s, "scalar", aT, mk.get("probs_free_T", 0))
                    aS.inc(s.activation(probs[:, :], ps_s[:, :], AF.Exp,
                                        bias=lcol[:, 128 + qc:129 + qc],
                                        accum_out=lcol[:, iq:iq + 1]))
                    mk["pss_free_S"] = aS.n
                blk.scalar(sexp)
                sem_ = aS.n

                def srec(v, iq=iq, sem_=sem_):
                    k.wait(v, "vector", aS, sem_)
                    aV.inc(v.reciprocal(lcol[:, 64 + iq:65 + iq],
                                        lcol[:, iq:iq + 1]))
                blk.vector(srec)
                recm = aV.n

                ptb = [ps_t0, ps_t1][iq % 2]

                def ptr_(t, ptb=ptb, iq=iq, sem_=sem_):
                    k.wait(t, "tensor", aS, sem_)
                    k.wait(t, "tensor", aV, mk.get(("ptb", iq % 2), 0))
                    mm = None
                    for kj in range(4):
                        mm = t.matmul(ptb[:, kj * 128:(kj + 1) * 128],
                                      probs[:, kj * 128:(kj + 1) * 128],
                                      c_["I128b"][:, :], start=(kj == 0),
                                      stop=(kj == 3), is_transpose=True)
                    aT.inc(mm)
                    mk["probs_free_T"] = aT.n
                blk.tensor(ptr_)
                ptm = aT.n

                def ptev(v, ptb=ptb, iq=iq, ptm=ptm):
                    k.wait(v, "vector", aT, ptm)
                    k.wait(v, "vector", aT, mk.get("pt_free_T", 0))
                    aV.inc(v.tensor_copy(pt[:, :], ptb[:, :]))
                    mk[("ptb", iq % 2)] = aV.n
                blk.vector(ptev)
                pvm = aV.n

                def pvmm(t, h=h, pvm=pvm):
                    k.wait(t, "tensor", aV, pvm)
                    k.wait(t, "tensor", aS, mk.get("psc_free_S", 0))
                    mm = None
                    for kj in range(4):
                        mm = t.matmul(
                            ps_c[:, :], pt[:, kj * 128:(kj + 1) * 128],
                            vwb[:, kj * 1024 + h * 64:kj * 1024 + (h + 1) * 64],
                            start=(kj == 0), stop=(kj == 3))
                    aT.inc(mm)
                    mk["pt_free_T"] = aT.n
                    if h == 15:
                        mk["vwb_free_T"] = aT.n
                blk.tensor(pvmm)
                cvm = aT.n

                def cev2(s, h=h, qc=qc, iq=iq, cvm=cvm, recm=recm, w=w):
                    k.wait(s, "scalar", aT, cvm)
                    k.wait(s, "scalar", aV, recm)
                    if w > 0:
                        k.wait(s, "scalar", aT, mk.get("ctxw_free_T", 0))
                    aS.inc(s.activation(
                        ctxw[:, qc * 1024 + h * 64:qc * 1024 + (h + 1) * 64],
                        ps_c[:, :], AF.Identity,
                        scale=lcol[:, 64 + iq:65 + iq]))
                    mk["psc_free_S"] = aS.n
                blk.scalar(cev2)
        ctx_ready = aS.n

        for tt in range(4):
            trow = w * 512 + tt * 128
            rb = tt % 2

            def ldres(sy, trow=trow, rb=rb):
                k.wait(sy, "sync", aW, mk.get(("xres_free", rb), 0))
                aW.dma(sy.dma_start(xres[rb][:, :], x_in[trow:trow + 128, :]))
            blk.sync(ldres)
            lrm = aW.n

            for grp in range(2):
                ptb = [ps_t0, ps_t1][grp]
                dstb = [pt, probs][grp]

                def ctr(t, tt=tt, grp=grp, ptb=ptb, ctx_ready=ctx_ready):
                    k.wait(t, "tensor", aS, ctx_ready)
                    k.wait(t, "tensor", aV, mk.get(("ptb", grp), 0))
                    if grp == 0:
                        k.wait(t, "tensor", aT, mk.get("ofm_free_T", 0))
                    mm = None
                    for q2 in range(4):
                        a = grp * 4 + q2
                        mm = t.matmul(
                            ptb[:, q2 * 128:(q2 + 1) * 128],
                            ctxw[:, tt * 1024 + a * 128:tt * 1024 + (a + 1) * 128],
                            c_["I128b"][:, :], start=(q2 == 0), stop=(q2 == 3),
                            is_transpose=True)
                    aT.inc(mm)
                    if tt == 3 and grp == 1:
                        mk["ctxw_free_T"] = aT.n
                blk.tensor(ctr)
                ctrm = aT.n

                def evc(v, grp=grp, ptb=ptb, dstb=dstb, ctrm=ctrm):
                    k.wait(v, "vector", aT, ctrm)
                    aV.inc(v.tensor_copy(dstb[:, :], ptb[:, :]))
                    mk[("ptb", grp)] = aV.n
                blk.vector(evc)
            ctf_ready = aV.n

            for oc in range(4):
                pb = ps_o[oc % 2]

                def omm(t, oc=oc, pb=pb, ctf_ready=ctf_ready, wc_ready=wc_ready):
                    k.wait(t, "tensor", aV, ctf_ready)
                    k.wait(t, "tensor", aG, wc_ready)
                    k.wait(t, "tensor", aV, mk.get(("pso_free", oc % 2), 0))
                    mm = None
                    for a in range(8):
                        srcb = pt if a < 4 else probs
                        t.matmul(
                            pb[:, :], srcb[:, (a % 4) * 128:(a % 4 + 1) * 128],
                            wdb[:, a * 2048 + oc * 512:a * 2048 + (oc + 1) * 512],
                            start=(a == 0), stop=False)
                    mm = t.matmul(pb[:, :], c_["ones1b"][:1, :],
                                  bdrow[:1, oc * 512:(oc + 1) * 512],
                                  start=False, stop=True)
                    aT.inc(mm)
                    if oc == 3:
                        mk["ofm_free_T"] = aT.n
                blk.tensor(omm)
                om = aT.n

                def oev(v, oc=oc, pb=pb, rb=rb, om=om, lrm=lrm):
                    k.wait(v, "vector", aT, om)
                    k.wait(v, "vector", aW, lrm)
                    aV.inc(v.scalar_tensor_tensor(
                        xres[rb][:, oc * 512:(oc + 1) * 512], pb[:, :], 1.0,
                        xres[rb][:, oc * 512:(oc + 1) * 512],
                        op0=ALU.mult, op1=ALU.add))
                    mk[("pso_free", oc % 2)] = aV.n
                blk.vector(oev)
            oevm = aV.n

            def stx1(sy, trow=trow, rb=rb, oevm=oevm):
                k.wait(sy, "sync", aV, oevm)
                aW.dma(sy.dma_start(x_out[trow:trow + 128, :], xres[rb][:, :]))
            blk.sync(stx1)
            mk[("xres_free", rb)] = aW.n

        mk["probs_free_T"] = aT.n
        mk["pt_free_T"] = aT.n

    es2.close()
    return aW.n


def emit_attn_b(k, blk, x_in, x_out, W, scr, sems):
    c_ = k.c
    nc = k.nc
    bW, bG, bT, bV, bS = (sems[x] for x in ("bW", "bG", "bT", "bV", "bS"))
    mk = {}
    es0 = ExitStack()
    sb0 = lambda n, sh, dt: es0.enter_context(nc.sbuf_tensor(n, list(sh), dt))
    stc = sb0("b_stc", (128, 40), F32)
    gb = sb0("b_gb", (128, 32), F32)

    esA = ExitStack()
    sbA = lambda n, sh, dt: esA.enter_context(nc.sbuf_tensor(n, list(sh), dt))
    psA = lambda n, sh: esA.enter_context(nc.psum_tensor(n, list(sh), F32))
    xnfv = sbA("b_xnfv", (128, 32768), BF16)
    kres = sbA("b_kres", (128, 32768), BF16)
    pb0 = psA("b_pb0", (128, 512))
    pb1 = psA("b_pb1", (128, 512))

    es1 = ExitStack()
    sb1 = lambda n, sh, dt: es1.enter_context(nc.sbuf_tensor(n, list(sh), dt))
    xt = [sb1(f"b_xt{i}", (128, 2048), F32) for i in range(2)]
    yb = [sb1(f"b_yb{i}", (128, 2048), F32) for i in range(2)]
    st = sb1("b_st", (128, 64), F32)
    gst = sb1("b_gst", (16, 256), F32)

    def ldgb(g):
        bG.dma(g.dma_start(gst[0:16, 0:128], AP(W["gamma"], 0, [[128, 16], [1, 128]])))
        bG.dma(g.dma_start(gst[0:16, 128:256], AP(W["beta"], 0, [[128, 16], [1, 128]])))
    blk.gpsimd(ldgb)
    gb_ld = bG.n

    def gbmm(t, gb_ld=gb_ld):
        k.wait(t, "tensor", bG, gb_ld)
        mm = None
        for i in range(2):
            mm = t.matmul(pb0[:, i * 16:(i + 1) * 16],
                          gst[0:16, i * 128:(i + 1) * 128],
                          c_["I128f"][0:16, 0:16], start=True, stop=True,
                          is_transpose=True)
        bT.inc(mm)
    blk.tensor(gbmm)
    gbm = bT.n

    def gbev(v, gbm=gbm):
        k.wait(v, "vector", bT, gbm)
        bV.inc(v.tensor_copy(gb[:, :], pb0[:, 0:32]))
    blk.vector(gbev)
    gb_ready = bV.n

    for tt in range(16):
        b = tt % 2
        trow = tt * 128
        s0 = (tt % 2) * 16

        def ldx(sy, b=b, trow=trow):
            k.wait(sy, "sync", bS, mk.get(("xt_free", b), 0))
            bW.dma(sy.dma_start(xt[b][:, :], x_in[trow:trow + 128, :]))
        blk.sync(ldx)
        ldm = bW.n

        def stats(v, b=b, s0=s0, ldm=ldm):
            k.wait(v, "vector", bW, ldm)
            bV.inc(v.tensor_reduce(st[:, s0:s0 + 1], xt[b][:, :],
                                   axis=AX.X, op=ALU.add))
        blk.vector(stats)
        sxm = bV.n

        def sumsq(s, b=b, s0=s0, ldm=ldm):
            k.wait(s, "scalar", bW, ldm)
            k.wait(s, "scalar", bT, mk.get(("yb_free", b), 0))
            bS.inc(s.activation(yb[b][:, :], xt[b][:, :], AF.Square,
                                accum_out=st[:, s0 + 1:s0 + 2]))
        blk.scalar(sumsq)
        sqm = bS.n

        def stats2(v, tt=tt, s0=s0, sxm=sxm, sqm=sqm):
            k.wait(v, "vector", bV, sxm)
            k.wait(v, "vector", bS, sqm)
            bV.inc(v.tensor_single_scalar(stc[:, tt:tt + 1], st[:, s0:s0 + 1],
                                          -1.0 / 2048.0, op=ALU.mult))
            k.wait(v, "vector", bV)
            bV.inc(v.tensor_tensor(st[:, s0 + 2:s0 + 3], stc[:, tt:tt + 1],
                                   stc[:, tt:tt + 1], op=ALU.mult))
            k.wait(v, "vector", bV)
            bV.inc(v.tensor_scalar(st[:, s0 + 3:s0 + 4], st[:, s0 + 1:s0 + 2],
                                   1.0 / 2048.0, None, op0=ALU.mult))
            k.wait(v, "vector", bV)
            bV.inc(v.tensor_tensor(st[:, s0 + 3:s0 + 4], st[:, s0 + 3:s0 + 4],
                                   st[:, s0 + 2:s0 + 3], op=ALU.subtract))
        blk.vector(stats2)
        varm = bV.n

        def rstd(s, tt=tt, s0=s0, varm=varm):
            k.wait(s, "scalar", bV, varm)
            bS.inc(s.activation(stc[:, 16 + tt:17 + tt], st[:, s0 + 3:s0 + 4],
                                AF.Ln, bias=c_["mgc"][:, 2:3]))
            k.wait(s, "scalar", bS)
            bS.inc(s.activation(stc[:, 16 + tt:17 + tt], stc[:, 16 + tt:17 + tt],
                                AF.Exp, scale=-0.5))
        blk.scalar(rstd)
        rsm = bS.n

        def yapply(v, b=b, tt=tt, rsm=rsm):
            k.wait(v, "vector", bS, rsm)
            bV.inc(v.tensor_scalar(yb[b][:, :], xt[b][:, :], stc[:, tt:tt + 1],
                                   stc[:, 16 + tt:17 + tt],
                                   op0=ALU.add, op1=ALU.mult))
        blk.vector(yapply)
        yam = bV.n

        def istats(v, b=b, s0=s0, yam=yam):
            k.wait(v, "vector", bV, yam)
            bV.inc(v.tensor_reduce(st[:, s0 + 4:s0 + 5], yb[b][:, :],
                                   axis=AX.X, op=ALU.add))
        blk.vector(istats)
        isxm = bV.n

        def isumsq(s, b=b, s0=s0, yam=yam):
            k.wait(s, "scalar", bV, yam)
            bS.inc(s.activation(xt[b][:, :], yb[b][:, :], AF.Square,
                                accum_out=st[:, s0 + 5:s0 + 6]))
            mk[("xt_free", b)] = bS.n
        blk.scalar(isumsq)
        isqm = bS.n

        def istats2(v, s0=s0, isxm=isxm, isqm=isqm):
            k.wait(v, "vector", bV, isxm)
            k.wait(v, "vector", bS, isqm)
            bV.inc(v.tensor_single_scalar(st[:, s0 + 6:s0 + 7],
                                          st[:, s0 + 4:s0 + 5],
                                          -1.0 / 2048.0, op=ALU.mult))
            k.wait(v, "vector", bV)
            bV.inc(v.tensor_tensor(st[:, s0 + 7:s0 + 8], st[:, s0 + 6:s0 + 7],
                                   st[:, s0 + 6:s0 + 7], op=ALU.mult))
            k.wait(v, "vector", bV)
            bV.inc(v.tensor_scalar(st[:, s0 + 8:s0 + 9], st[:, s0 + 5:s0 + 6],
                                   1.0 / 2048.0, None, op0=ALU.mult))
            k.wait(v, "vector", bV)
            bV.inc(v.tensor_tensor(st[:, s0 + 8:s0 + 9], st[:, s0 + 8:s0 + 9],
                                   st[:, s0 + 7:s0 + 8], op=ALU.subtract))
        blk.vector(istats2)
        ivarm = bV.n

        def irstd(s, s0=s0, ivarm=ivarm):
            k.wait(s, "scalar", bV, ivarm)
            bS.inc(s.activation(st[:, s0 + 9:s0 + 10], st[:, s0 + 8:s0 + 9],
                                AF.Ln, bias=c_["mgc"][:, 2:3]))
            k.wait(s, "scalar", bS)
            bS.inc(s.activation(st[:, s0 + 9:s0 + 10], st[:, s0 + 9:s0 + 10],
                                AF.Exp, scale=-0.5))
        blk.scalar(irstd)
        irsm = bS.n

        def iapply(v, b=b, s0=s0, irsm=irsm):
            k.wait(v, "vector", bS, irsm)
            bV.inc(v.tensor_scalar(yb[b][:, :], yb[b][:, :],
                                   st[:, s0 + 6:s0 + 7], st[:, s0 + 9:s0 + 10],
                                   op0=ALU.add, op1=ALU.mult))
        blk.vector(iapply)
        iam = bV.n

        for g4 in range(4):
            pb = [pb0, pb1][g4 % 2]

            def trs(t, b=b, g4=g4, pb=pb, iam=iam):
                k.wait(t, "tensor", bV, iam)
                k.wait(t, "tensor", bV, mk.get(("pbb", g4 % 2), 0))
                mm = None
                for q2 in range(4):
                    a = g4 * 4 + q2
                    mm = t.matmul(pb[:, q2 * 128:(q2 + 1) * 128],
                                  yb[b][:, a * 128:(a + 1) * 128],
                                  c_["I128f"][:, :], start=(q2 == 0),
                                  stop=(q2 == 3), is_transpose=True)
                bT.inc(mm)
                if g4 == 3:
                    mk[("yb_free", b)] = bT.n
            blk.tensor(trs)
            trm = bT.n

            def evx(v, g4=g4, pb=pb, tt=tt, trm=trm):
                k.wait(v, "vector", bT, trm)
                src = AP(pb, 0, [[512, 128], [128, 4], [1, 128]])
                dst = AP(xnfv, (g4 * 4) * 2048 + tt * 128,
                         [[32768, 128], [2048, 4], [1, 128]])
                bV.inc(v.tensor_copy(dst, src))
                mk[("pbb", g4 % 2)] = bV.n
            blk.vector(evx)
    xn_raw = bV.n
    c1a_S = bS.n
    c1a_T = bT.n

    def affine(v, xn_raw=xn_raw, gb_ready=gb_ready):
        k.wait(v, "vector", bV, xn_raw)
        k.wait(v, "vector", bV, gb_ready)
        mmv = None
        for ft in range(16):
            mmv = v.tensor_scalar(xnfv[:, ft * 2048:(ft + 1) * 2048],
                                  xnfv[:, ft * 2048:(ft + 1) * 2048],
                                  gb[:, ft:ft + 1], gb[:, 16 + ft:17 + ft],
                                  op0=ALU.mult, op1=ALU.add)
        bV.inc(mmv)
    blk.vector(affine)
    xn_ready = bV.n

    es1.close()
    es1b = ExitStack()
    sb1b = lambda n, sh, dt: es1b.enter_context(
        nc.sbuf_tensor(n, list(sh), dt))
    wkb = [sb1b(f"b_wkb{i}", (128, 2048), BF16) for i in range(2)]
    qst = sb1b("b_qst", (128, 2048), BF16)
    for tag, coff in (("k", 2048), ("q", 0)):
        for kf in range(16):
            wb = kf % 2

            def ldw(g, kf=kf, wb=wb, coff=coff):
                k.wait(g, "gpsimd", bV, xn_ready)
                k.wait(g, "gpsimd", bS, c1a_S)
                k.wait(g, "gpsimd", bT, c1a_T)
                k.wait(g, "gpsimd", bT, mk.get(("wkb_free", wb), 0))
                src = AP(W["qkv"], coff + kf * 128,
                         [[6144, 128], [128 * 6144, 16], [1, 128]])
                bG.dma(g.dma_start(wkb[wb][:, :], src))
            blk.gpsimd(ldw)
            wlm = bG.n

            for tq in range(4):
                pb = [pb0, pb1][tq % 2]

                def pmm(t, kf=kf, wb=wb, tq=tq, pb=pb, wlm=wlm,
                        xn_ready=xn_ready):
                    k.wait(t, "tensor", bV, xn_ready)
                    k.wait(t, "tensor", bG, wlm)
                    k.wait(t, "tensor", bV, mk.get(("pbb", tq % 2), 0))
                    k.wait(t, "tensor", bS, mk.get(("pbbs", tq % 2), 0))
                    mm = None
                    for ft in range(16):
                        mm = t.matmul(
                            pb[:, :], wkb[wb][:, ft * 128:(ft + 1) * 128],
                            xnfv[:, ft * 2048 + tq * 512:ft * 2048 + (tq + 1) * 512],
                            start=(ft == 0), stop=(ft == 15))
                    bT.inc(mm)
                    if tq == 3:
                        mk[("wkb_free", wb)] = bT.n
                blk.tensor(pmm)
                pm = bT.n

                if tag == "k":
                    def evk(v, kf=kf, tq=tq, pb=pb, pm=pm):
                        k.wait(v, "vector", bT, pm)
                        bV.inc(v.tensor_copy(
                            kres[:, kf * 2048 + tq * 512:kf * 2048 + (tq + 1) * 512],
                            pb[:, :]))
                        mk[("pbb", tq % 2)] = bV.n
                    blk.vector(evk)
                else:
                    def evq(s, kf=kf, tq=tq, pb=pb, pm=pm):
                        k.wait(s, "scalar", bT, pm)
                        k.wait(s, "scalar", bW, mk.get("qst_free", 0))
                        bS.inc(s.activation(qst[:, tq * 512:(tq + 1) * 512],
                                            pb[:, :], AF.Identity,
                                            scale=c_["mgc"][:, 5:6]))
                        mk[("pbbs", tq % 2)] = bS.n
                    blk.scalar(evq)
            if tag == "q":
                qem = bS.n

                def spq(sy, kf=kf, qem=qem):
                    k.wait(sy, "sync", bS, qem)
                    dst = AP(scr["qsc"], kf * 128 * 2048, [[2048, 128], [1, 2048]])
                    bW.dma(sy.dma_start(dst, qst[:, :]))
                    mk["qst_free"] = bW.n
                blk.sync(spq)
    kq_done = bT.n

    for ft in range(16):
        def spxn(sy, ft=ft, kq_done=kq_done):
            k.wait(sy, "sync", bT, kq_done)
            dst = AP(scr["xnsc"], ft * 128 * 2048, [[2048, 128], [1, 2048]])
            bW.dma(sy.dma_start(dst, xnfv[:, ft * 2048:(ft + 1) * 2048]))
        blk.sync(spxn)
    xn_spilled = bW.n

    es1b.close()
    es1c = ExitStack()
    sb1c = lambda n, sh, dt: es1c.enter_context(
        nc.sbuf_tensor(n, list(sh), dt))
    wvb = sb1c("b_wvb", (128, 16 * 1024), BF16)
    xv = [sb1c(f"b_xv{i}", (128, 2048), BF16) for i in range(2)]
    for hh in range(2):
        def ldwv(g, hh=hh):
            k.wait(g, "gpsimd", bT, kq_done)
            k.wait(g, "gpsimd", bW, xn_spilled)
            k.wait(g, "gpsimd", bT, mk.get("wvb_free", 0))
            src = AP(W["qkv"], 4096 + hh * 1024,
                     [[6144, 128], [128 * 6144, 16], [1, 1024]])
            bG.dma(g.dma_start(wvb[:, :], src))
        blk.gpsimd(ldwv)
        wvm = bG.n

        for tt in range(16):
            b = tt % 2

            def ldxv(sy, tt=tt, b=b, xn_spilled=xn_spilled):
                k.wait(sy, "sync", bW, xn_spilled)
                k.wait(sy, "sync", bT, mk.get(("xv_free", b), 0))
                src = AP(scr["xnsc"], tt * 128,
                         [[2048, 128], [128 * 2048, 16], [1, 128]])
                bW.dma(sy.dma_start(xv[b][:, :], src))
            blk.sync(ldxv)
            xvm = bW.n

            for vo in range(2):
                pb = [pb0, pb1][vo]

                def vmm(t, tt=tt, b=b, vo=vo, pb=pb, xvm=xvm, wvm=wvm,
                        xn_spilled_=xn_spilled):
                    k.wait(t, "tensor", bW, xvm)
                    k.wait(t, "tensor", bG, wvm)
                    k.wait(t, "tensor", bV, mk.get(("pbb", vo), 0))
                    mm = None
                    for ft in range(16):
                        mm = t.matmul(
                            pb[:, :], xv[b][:, ft * 128:(ft + 1) * 128],
                            wvb[:, ft * 1024 + vo * 512:ft * 1024 + (vo + 1) * 512],
                            start=(ft == 0), stop=(ft == 15))
                    bT.inc(mm)
                    if vo == 1:
                        mk[("xv_free", b)] = bT.n
                    if tt == 15 and vo == 1:
                        mk["wvb_free"] = bT.n
                blk.tensor(vmm)
                vm = bT.n

                def evv(v, tt=tt, hh=hh, vo=vo, pb=pb, vm=vm):
                    k.wait(v, "vector", bT, vm)
                    bV.inc(v.tensor_copy(
                        xnfv[:, tt * 2048 + hh * 1024 + vo * 512:
                             tt * 2048 + hh * 1024 + (vo + 1) * 512],
                        pb[:, :]))
                    mk[("pbb", vo)] = bV.n
                blk.vector(evv)
    v_ready = bV.n
    c1c_T = bT.n
    c1c_W = bW.n
    es1c.close()

    es2 = ExitStack()
    sb2 = lambda n, sh, dt: es2.enter_context(nc.sbuf_tensor(n, list(sh), dt))
    ps2 = lambda n, sh, dt=F32: es2.enter_context(
        nc.psum_tensor(n, list(sh), dt))
    qb = [sb2(f"b_qb{i}", (128, 2048), BF16) for i in range(2)]
    probs = sb2("b_probs", (128, 2048), BF16)
    pT = sb2("b_pT", (128, 2048), BF16)
    ctxst = sb2("b_ctxst", (128, 2048), BF16)
    ctxf = sb2("b_ctxf", (128, 2048), BF16)
    cmb = sb2("b_cmb", (128, 2048), BF16)
    lc = sb2("b_lc", (128, 16 * 16), F32)
    pbs = [ps2(f"b_pbs{i}", (128, 512)) for i in range(4)]
    pbt = [ps2(f"b_pbt{i}", (128, 512), BF16) for i in range(2)]

    def ldcm(g):
        k.wait(g, "gpsimd", bT, c1c_T)
        k.wait(g, "gpsimd", bW, c1c_W)
        for qc in range(4):
            bG.dma(g.dma_start(cmb[:, qc * 512:(qc + 1) * 512],
                               W["cmask"][qc]))
    blk.gpsimd(ldcm)
    cm_ld = bG.n
    mk[("qb_free", 0)] = c1c_W
    mk[("qb_free", 1)] = c1c_W

    for qi in range(16):
        b = qi % 2
        nch = qi // 4 + 1
        L0 = qi * 16

        def ldq(sy, qi=qi, b=b):
            k.wait(sy, "sync", bW, mk.get(("qb_free", b), 0))
            src = AP(scr["qsc"], qi * 128,
                     [[2048, 128], [128 * 2048, 16], [1, 128]])
            bW.dma(sy.dma_start(qb[b][:, :], src))
        blk.sync(ldq)
        qlm = bW.n

        exp_marks = []
        mx_marks = []
        for ch in range(nch):
            pb = pbs[ch]

            def smm(t, qi=qi, b=b, ch=ch, nch=nch, pb=pb, qlm=qlm,
                    v_ready=v_ready, cm_ld=cm_ld):
                k.wait(t, "tensor", bW, qlm)
                k.wait(t, "tensor", bV, v_ready)
                k.wait(t, "tensor", bG, cm_ld)
                k.wait(t, "tensor", bS, mk.get(("pbs_free", ch), 0))
                diag = (ch == nch - 1)
                mm = None
                for kf in range(16):
                    mm = t.matmul(
                        pb[:, :], qb[b][:, kf * 128:(kf + 1) * 128],
                        kres[:, kf * 2048 + ch * 512:kf * 2048 + (ch + 1) * 512],
                        start=(kf == 0), stop=(kf == 15 and not diag))
                if diag:
                    mm = t.matmul(pb[:, :], c_["I128b"][:, :],
                                  cmb[:, (qi % 4) * 512:(qi % 4 + 1) * 512],
                                  start=False, stop=True)
                bT.inc(mm)
            blk.tensor(smm)
            sm = bT.n

            def smax(v, ch=ch, pb=pb, L0=L0, sm=sm):
                k.wait(v, "vector", bT, sm)
                bV.inc(v.tensor_reduce(lc[:, L0 + ch:L0 + ch + 1], pb[:, :],
                                       axis=AX.X, op=ALU.max))
            blk.vector(smax)
            mx_marks.append(bV.n)

        def mxcomb(v, nch=nch, L0=L0, ms=tuple(mx_marks)):
            k.wait(v, "vector", bV, ms[-1])
            if nch == 1:
                bV.inc(v.tensor_single_scalar(lc[:, L0 + 4:L0 + 5],
                                              lc[:, L0:L0 + 1], -1.0,
                                              op=ALU.mult))
            else:
                mmv = v.tensor_tensor(lc[:, L0 + 4:L0 + 5], lc[:, L0:L0 + 1],
                                      lc[:, L0 + 1:L0 + 2], op=ALU.max)
                bV.inc(mmv)
                for ch in range(2, nch):
                    k.wait(v, "vector", bV)
                    bV.inc(v.tensor_tensor(lc[:, L0 + 4:L0 + 5],
                                           lc[:, L0 + 4:L0 + 5],
                                           lc[:, L0 + ch:L0 + ch + 1],
                                           op=ALU.max))
                k.wait(v, "vector", bV)
                bV.inc(v.tensor_single_scalar(lc[:, L0 + 4:L0 + 5],
                                              lc[:, L0 + 4:L0 + 5], -1.0,
                                              op=ALU.mult))
        blk.vector(mxcomb)
        negm_mark = bV.n

        for ch in range(nch):
            pb = pbs[ch]

            def sexp(s, ch=ch, pb=pb, L0=L0, negm_mark=negm_mark, qi=qi):
                k.wait(s, "scalar", bV, negm_mark)
                k.wait(s, "scalar", bT, mk.get("probs_free_T", 0))
                bS.inc(s.activation(probs[:, ch * 512:(ch + 1) * 512],
                                    pb[:, :], AF.Exp,
                                    bias=lc[:, L0 + 4:L0 + 5],
                                    accum_out=lc[:, L0 + 8 + ch:L0 + 9 + ch]))
            blk.scalar(sexp)
            exp_marks.append(bS.n)

        def lsum(v, nch=nch, L0=L0, ms=tuple(exp_marks)):
            k.wait(v, "vector", bS, ms[-1])
            if nch == 1:
                bV.inc(v.reciprocal(lc[:, L0 + 5:L0 + 6],
                                    lc[:, L0 + 8:L0 + 9]))
            else:
                bV.inc(v.tensor_tensor(lc[:, L0 + 6:L0 + 7],
                                       lc[:, L0 + 8:L0 + 9],
                                       lc[:, L0 + 9:L0 + 10], op=ALU.add))
                for ch in range(2, nch):
                    k.wait(v, "vector", bV)
                    bV.inc(v.tensor_tensor(lc[:, L0 + 6:L0 + 7],
                                           lc[:, L0 + 6:L0 + 7],
                                           lc[:, L0 + 8 + ch:L0 + 9 + ch],
                                           op=ALU.add))
                k.wait(v, "vector", bV)
                bV.inc(v.reciprocal(lc[:, L0 + 5:L0 + 6], lc[:, L0 + 6:L0 + 7]))
        blk.vector(lsum)
        linv_mark = bV.n

        for ch in range(nch):
            ptb = pbt[ch % 2]

            def ptr_(t, ch=ch, ptb=ptb, em=exp_marks[ch]):
                k.wait(t, "tensor", bS, exp_marks[-1])
                k.wait(t, "tensor", bV, mk.get(("pbt_free", ch % 2), 0))
                mm = None
                for j2 in range(4):
                    mm = t.matmul(ptb[:, j2 * 128:(j2 + 1) * 128],
                                  probs[:, ch * 512 + j2 * 128:ch * 512 + (j2 + 1) * 128],
                                  c_["I128b"][:, :], start=(j2 == 0),
                                  stop=(j2 == 3), is_transpose=True)
                bT.inc(mm)
                mk["probs_free_T"] = bT.n
            blk.tensor(ptr_)
            ptm = bT.n

            def ptev(v, ch=ch, ptb=ptb, ptm=ptm):
                k.wait(v, "vector", bT, ptm)
                bV.inc(v.tensor_copy(pT[:, ch * 512:(ch + 1) * 512], ptb[:, :]))
                mk[("pbt_free", ch % 2)] = bV.n
            blk.vector(ptev)
        pt_ready = bV.n

        for vo in range(4):
            pb = pbs[vo]

            def pvmm(t, vo=vo, pb=pb, nch=nch, pt_ready=pt_ready,
                     em=exp_marks[-1]):
                k.wait(t, "tensor", bV, pt_ready)
                k.wait(t, "tensor", bS, em)
                k.wait(t, "tensor", bS, mk.get(("pbs_frees", vo), 0))
                mm = None
                for jt in range(nch * 4):
                    mm = t.matmul(pb[:, :], pT[:, jt * 128:(jt + 1) * 128],
                                  xnfv[:, jt * 2048 + vo * 512:jt * 2048 + (vo + 1) * 512],
                                  start=(jt == 0), stop=(jt == nch * 4 - 1))
                bT.inc(mm)
            blk.tensor(pvmm)
            cvm = bT.n

            def cev(s, vo=vo, pb=pb, cvm=cvm, L0=L0, linv_mark=linv_mark):
                k.wait(s, "scalar", bT, cvm)
                k.wait(s, "scalar", bT, mk.get("ctxst_free_T", 0))
                bS.inc(s.activation(ctxst[:, vo * 512:(vo + 1) * 512], pb[:, :],
                                    AF.Identity, scale=lc[:, L0 + 5:L0 + 6]))
                mk[("pbs_free", vo)] = bS.n
                mk[("pbs_frees", vo)] = bS.n
            blk.scalar(cev)
        ctx_tm = bS.n

        for g4 in range(4):
            ptb = pbt[g4 % 2]

            def ctr(t, g4=g4, ptb=ptb, ctx_tm=ctx_tm):
                k.wait(t, "tensor", bS, ctx_tm)
                k.wait(t, "tensor", bV, mk.get(("pbt_free", g4 % 2), 0))
                mm = None
                for q2 in range(4):
                    a = g4 * 4 + q2
                    mm = t.matmul(ptb[:, q2 * 128:(q2 + 1) * 128],
                                  ctxst[:, a * 128:(a + 1) * 128],
                                  c_["I128b"][:, :], start=(q2 == 0),
                                  stop=(q2 == 3), is_transpose=True)
                bT.inc(mm)
                if g4 == 3:
                    mk["ctxst_free_T"] = bT.n
            blk.tensor(ctr)
            ctm = bT.n

            def cfev(v, g4=g4, ptb=ptb, ctm=ctm):
                k.wait(v, "vector", bT, ctm)
                k.wait(v, "vector", bW, mk.get("ctxf_free_W", 0))
                bV.inc(v.tensor_copy(ctxf[:, g4 * 512:(g4 + 1) * 512],
                                     ptb[:, :]))
                mk[("pbt_free", g4 % 2)] = bV.n
            blk.vector(cfev)
        cf_mark = bV.n

        def spc(sy, qi=qi, cf_mark=cf_mark):
            k.wait(sy, "sync", bV, cf_mark)
            dst = AP(scr["csc"], qi * 128 * 2048, [[2048, 128], [1, 2048]])
            bW.dma(sy.dma_start(dst, ctxf[:, :]))
            mk[("qb_free", qi % 2)] = bW.n
            mk["ctxf_free_W"] = bW.n
        blk.sync(spc)
    c2_T = bT.n
    c2_W = bW.n
    es2.close()
    esA.close()

    es3 = ExitStack()
    sb3 = lambda n, sh, dt: es3.enter_context(nc.sbuf_tensor(n, list(sh), dt))
    ps3 = lambda n, sh, dt=F32: es3.enter_context(
        nc.psum_tensor(n, list(sh), dt))
    opb = sb3("b_opb", (128, 32768), BF16)
    ctb = [sb3(f"b_ctb{i}", (128, 2048), BF16) for i in range(2)]
    xt2 = [sb3(f"b_xt2{i}", (128, 2048), F32) for i in range(2)]
    ob = sb3("b_ob", (128, 2048), F32)
    po = [ps3(f"b_po{i}", (128, 512)) for i in range(2)]

    def ldop(g):
        k.wait(g, "gpsimd", bT, c2_T)
        k.wait(g, "gpsimd", bW, c2_W)
        src = AP(W["oproj"], 0, [[2048, 128], [128 * 2048, 16], [1, 2048]])
        bG.dma(g.dma_start(opb[:, :], src))
    blk.gpsimd(ldop)
    op_ld = bG.n
    mk[("ctb_free", 0)] = c2_T
    mk[("ctb_free", 1)] = c2_T
    mk["ob_free_W"] = c2_W

    for tt in range(16):
        b = tt % 2
        trow = tt * 128

        def ldc3(sy, tt=tt, b=b):
            k.wait(sy, "sync", bW, c2_W)
            k.wait(sy, "sync", bT, mk.get(("ctb_free", b), 0))
            src = AP(scr["csc"], tt * 128 * 2048, [[2048, 128], [1, 2048]])
            bW.dma(sy.dma_start(ctb[b][:, :], src))
            bW.dma(sy.dma_start(xt2[b][:, :], x_in[trow:trow + 128, :]))
        blk.sync(ldc3)
        lcm = bW.n

        def yrec(v, b=b, tt=tt, lcm=lcm):
            k.wait(v, "vector", bW, lcm)
            k.wait(v, "vector", bW, mk.get("ob_free_W", 0))
            v.tensor_scalar(ob[:, :], xt2[b][:, :], stc[:, tt:tt + 1],
                            stc[:, 16 + tt:17 + tt], op0=ALU.add, op1=ALU.mult)
            bV.inc(v.tensor_tensor(ob[:, :], ob[:, :], xt2[b][:, :],
                                   op=ALU.add))
        blk.vector(yrec)
        yrm = bV.n

        for hc in range(4):
            pb = po[hc % 2]

            def omm(t, b=b, hc=hc, pb=pb, lcm=lcm, op_ld=op_ld):
                k.wait(t, "tensor", bW, lcm)
                k.wait(t, "tensor", bG, op_ld)
                k.wait(t, "tensor", bV, mk.get(("po_free", hc % 2), 0))
                mm = None
                for vft in range(16):
                    mm = t.matmul(
                        pb[:, :], ctb[b][:, vft * 128:(vft + 1) * 128],
                        opb[:, vft * 2048 + hc * 512:vft * 2048 + (hc + 1) * 512],
                        start=(vft == 0), stop=(vft == 15))
                bT.inc(mm)
                if hc == 3:
                    mk[("ctb_free", b)] = bT.n
            blk.tensor(omm)
            om = bT.n

            def oev(v, hc=hc, pb=pb, om=om, yrm=yrm):
                k.wait(v, "vector", bT, om)
                k.wait(v, "vector", bV, yrm)
                bV.inc(v.scalar_tensor_tensor(
                    ob[:, hc * 512:(hc + 1) * 512], pb[:, :], 1.0,
                    ob[:, hc * 512:(hc + 1) * 512], op0=ALU.mult, op1=ALU.add))
                mk[("po_free", hc % 2)] = bV.n
            blk.vector(oev)
        oevm = bV.n

        def stx(sy, trow=trow, oevm=oevm):
            k.wait(sy, "sync", bV, oevm)
            bW.dma(sy.dma_start(x_out[trow:trow + 128, :], ob[:, :]))
            mk["ob_free_W"] = bW.n
        blk.sync(stx)

    es3.close()
    es0.close()
    return bW.n


def emit_barrier(k, blk, semlist):
    vals = tuple((s, s.n) for s in semlist)
    for eng in ("sync", "gpsimd", "scalar", "vector", "tensor"):
        def bf(e, eng=eng, vals=vals):
            for s, v in vals:
                k.wait(e, eng, s, v)
        getattr(blk, eng)(bf)


def build_program():
    nc = bass.Bass("TRN2", target_bir_lowering=False)
    k = K(nc)
    consts = host_constants()
    dram_c = {}
    for name, shape, dt in CONST_SPECS:
        dram_c[name] = nc.dram_tensor("dc_" + name, list(consts[name].shape),
                                      dt, kind="ExternalInput").ap()

    def din(name, shape):
        return nc.dram_tensor(name, list(shape), F32,
                              kind="ExternalInput").ap()

    x = din("L0_attn_a_hidden_states", (2048, 2048))
    Wa = {"wc": din("L0_attn_a_Wc", (2048, 1024)),
          "bc": din("L0_attn_a_bc", (1024,)),
          "wq": din("L0_attn_a_Wq", (1024, 1024)),
          "bq": din("L0_attn_a_bq", (1024,)),
          "wk": din("L0_attn_a_Wk", (1024, 1024)),
          "bk": din("L0_attn_a_bk", (1024,)),
          "wv": din("L0_attn_a_Wv", (1024, 1024)),
          "bv": din("L0_attn_a_bv", (1024,)),
          "relb": din("L0_attn_a_rel_bias", (257,)),
          "wd": din("L0_attn_a_Wd", (1024, 2048)),
          "bd": din("L0_attn_a_bd", (2048,)),
          "vd": nc.dram_tensor("s_vd", [1023], F32).ap()}
    Wb = {"qkv": din("L0_attn_b_qkv", (2048, 6144)),
          "oproj": din("L0_attn_b_o_proj", (2048, 2048)),
          "gamma": din("L0_attn_b_gamma", (2048,)),
          "beta": din("L0_attn_b_beta", (2048,)),
          "cmask": dram_c["cmask"]}
    wg = din("L0_ffn_w_gate", (8192, 2048))
    wu = din("L0_ffn_w_up", (8192, 2048))
    wdn = din("L0_ffn_w_down", (2048, 8192))
    out = nc.dram_tensor("out", [2048, 2048], F32, kind="ExternalOutput").ap()

    x1 = nc.dram_tensor("s_x1", [2048, 2048], F32).ap()
    x2 = nc.dram_tensor("s_x2", [2048, 2048], F32).ap()
    x3 = nc.dram_tensor("s_x3", [2048, 2048], F32).ap()
    scrb = {"xnsc": nc.dram_tensor("s_xnsc", [2048, 2048], BF16).ap(),
            "qsc": nc.dram_tensor("s_qsc", [2048, 2048], BF16).ap(),
            "csc": nc.dram_tensor("s_csc", [2048, 2048], BF16).ap()}
    scrf = {"g": nc.dram_tensor("s_sg", [64 * 16 * 128, 128], BF16).ap(),
            "u": nc.dram_tensor("s_su", [64 * 16 * 128, 128], BF16).ap(),
            "d": nc.dram_tensor("s_sd", [64 * 16 * 128, 128], BF16).ap()}

    with k.es:
        wsid = k.sb("wsid", (128, 8), F32)
        with nc.Block() as blk:
            emit_load_consts(k, blk, dram_c)

        asems = {n: k.sem(n) for n in ("aW", "aG", "aT", "aV", "aS")}
        with nc.Block() as blk:
            emit_attn_a(k, blk, x, x1, Wa, asems)
        with nc.Block() as blk:
            emit_barrier(k, blk, list(asems.values()))

        with nc.Block() as blk:
            _, psems = emit_prep(k, blk,
                                 {"g": (wg, 8192, 2048), "u": (wu, 8192, 2048),
                                  "d": (wdn, 2048, 8192)}, scrf, wsid)
        with nc.Block() as blk:
            emit_barrier(k, blk, psems)

        esF = ExitStack()
        B1 = alloc_ffn(k, esF, "1")
        fs1 = {n: k.sem(n + "1") for n in ("fW", "fG", "fT", "fV", "fS")}
        with nc.Block() as blk:
            emit_ffn_app(k, blk, B1, x1, x2, scrf, wsid, fs1)
        with nc.Block() as blk:
            emit_barrier(k, blk, [fs1[n] for n in ("fW", "fG", "fT", "fV", "fS")])
        esF.close()

        bsems = {n: k.sem(n) for n in ("bW", "bG", "bT", "bV", "bS")}
        with nc.Block() as blk:
            emit_attn_b(k, blk, x2, x3, Wb, scrb, bsems)
        with nc.Block() as blk:
            emit_barrier(k, blk, list(bsems.values()))

        esF2 = ExitStack()
        B2 = alloc_ffn(k, esF2, "2")
        fs2 = {n: k.sem(n + "2") for n in ("fW", "fG", "fT", "fV", "fS")}
        with nc.Block() as blk:
            emit_ffn_app(k, blk, B2, x3, out, scrf, wsid, fs2)
            def fin(g):
                k.wait(g, "gpsimd", fs2["fG"])
            blk.gpsimd(fin)
        esF2.close()
    return nc


_INPUT_NAMES = [
    "L0_attn_a_hidden_states", "L0_attn_a_Wc", "L0_attn_a_bc",
    "L0_attn_a_Wq", "L0_attn_a_bq", "L0_attn_a_Wk", "L0_attn_a_bk",
    "L0_attn_a_Wv", "L0_attn_a_bv", "L0_attn_a_rel_bias", "L0_attn_a_Wd",
    "L0_attn_a_bd", "L0_attn_b_qkv", "L0_attn_b_o_proj", "L0_attn_b_gamma",
    "L0_attn_b_beta", "L0_ffn_w_gate", "L0_ffn_w_up", "L0_ffn_w_down",
]


def kernel(**inputs):
    consts = host_constants()
    nc = build_program()
    shared = {}
    for name in _INPUT_NAMES:
        if name == "L0_attn_a_hidden_states":
            continue
        shared[name] = np.ascontiguousarray(
            np.asarray(inputs[name], dtype=np.float32))
    for name, shape, dt in CONST_SPECS:
        shared["dc_" + name] = consts[name]
    xfull = np.asarray(inputs["L0_attn_a_hidden_states"], dtype=np.float32)
    in_maps = []
    for core in range(8):
        m = dict(shared)
        m["L0_attn_a_hidden_states"] = np.ascontiguousarray(xfull[core])
        in_maps.append(m)
    res = run_bass_kernel_spmd(nc, in_maps, core_ids=list(range(8)))
    outs = [np.asarray(res.results[i]["out"], dtype=np.float32)
            for i in range(8)]
    return np.stack(outs, axis=0)
